# version 18
# baseline (speedup 1.0000x reference)
import numpy as np
import ml_dtypes
from contextlib import ExitStack

import concourse.bass as bass
import concourse.mybir as mybir
from concourse.bass_utils import run_bass_kernel_spmd

F32 = mybir.dt.float32
BF16 = mybir.dt.bfloat16
AF = mybir.ActivationFunctionType
ALU = mybir.AluOpType
AX = mybir.AxisListType

P = 128
D = 1024
H = 64
T = 2048
NT = T // P
NC = D // P
B = 8

F1 = np.float32

DEBUG = False

OFF = {"A1": 0, "A2": 1, "A3": 2, "A4": 3, "A5": 4,
       "T1": 5, "G1": 5, "T2": 5, "G2": 5,
       "QK": 6, "PH": 7, "TR": 8, "SE": 9, "AM": 9,
       "OM": 10, "ON": 11, "OW": 12, "E": 13, "EO": 14}


class Trk:

    def __init__(self):
        self.E = {}
        self.planning = True
        self.n = 0

    def reset(self, planning):
        self.planning = planning
        self.n = 0

    def ev(self, key):
        self.n += 1
        if self.planning:
            self.E[key] = self.n
        else:
            assert self.E[key] == self.n, (key, self.E[key], self.n)
        return self.n


def make_sched():
    wabs_at = {2: [0, 1, 2], 3: [3, 4, 5], 4: [6, 7]}
    wtern_at = {6: [0, 4], 7: [1, 5], 8: [2, 6], 9: [3, 7]}
    wsub_at = {8: [0, 4], 9: [1, 5], 10: [2, 6], 11: [3, 7]}
    iters = [[] for _ in range(NT + 17)]
    for t in range(NT):
        for s, o in OFF.items():
            iters[t + o].append((s, t))
    for i, cs in wabs_at.items():
        for c in cs:
            iters[i].append(("Wabs", c))
    iters[5].append(("Wdelta", 0))
    for i, cs in wtern_at.items():
        for c in cs:
            iters[i].append(("Wtern", c))
    for i, cs in wsub_at.items():
        for c in cs:
            iters[i].append(("Wsub", c))
    return iters


def build_nc():
    nc = bass.Bass("TRN2", target_bir_lowering=False)
    dbg_d = {}
    if DEBUG:
        dbg_d["dbg_wT"] = nc.dram_tensor("dbg_wT", [P, NC * D], BF16, kind="ExternalOutput").ap()
        dbg_d["dbg_xnfm"] = nc.dram_tensor("dbg_xnfm", [P, NT * D], BF16, kind="ExternalOutput").ap()
        dbg_d["dbg_o1T"] = nc.dram_tensor("dbg_o1T", [H + 1, NT * P], BF16, kind="ExternalOutput").ap()
        dbg_d["dbg_draw"] = nc.dram_tensor("dbg_draw", [P, 1], F32, kind="ExternalOutput").ap()
        dbg_d["dbg_stats"] = nc.dram_tensor("dbg_stats", [P, 48], F32, kind="ExternalOutput").ap()

    x_d = nc.dram_tensor("x", [T, D], F32, kind="ExternalInput").ap()
    wf_d = nc.dram_tensor("wf", [D, D], F32, kind="ExternalInput").ap()
    wqkv_d = nc.dram_tensor("wqkv", [D, 3 * H], F32, kind="ExternalInput").ap()
    wob_d = nc.dram_tensor("wob", [H + 1, D], F32, kind="ExternalInput").ap()
    alpha_d = nc.dram_tensor("alpha", [P, 1], F32, kind="ExternalInput").ap()
    id_d = nc.dram_tensor("ident", [P, P], BF16, kind="ExternalInput").ap()
    mask_d = nc.dram_tensor("maskt", [P, P], BF16, kind="ExternalInput").ap()
    out_d = nc.dram_tensor("out", [T, D], F32, kind="ExternalOutput").ap()

    trk = {"te": Trk(), "dve": Trk(), "sc": Trk(), "gp": Trk()}
    iters = make_sched()
    PRIO = {
        "te": {"T1": 1, "QK": 2, "TR": 3, "AM": 4, "OM": 5, "ON": 6,
               "T2": 7, "Wdelta": 8, "OW": 9},
        "dve": {"ON": 0, "PH": 1, "G1": 2, "E": 3, "AM": 4,
                "A5": 5, "A3": 6, "A1": 7,
                "Wabs": 9, "Wtern": 10, "Wsub": 11, "Wdelta": 12},
        "sc": {"PH": 0, "SE": 1, "ON": 2, "OW": 3, "A4": 4, "A2": 5,
               "A1": 6, "G2": 7, "Wtern": 8, "Wdelta": 9, "Wabs": 10},
        "gp": {"Wdelta": 0, "EO": 1},
    }
    eng_sched = {}
    for eng, pr in PRIO.items():
        seq = []
        for it in iters:
            seq.extend(sorted((ent for ent in it if ent[0] in pr),
                              key=lambda ent: pr[ent[0]]))
        eng_sched[eng] = seq

    with ExitStack() as ctx:
        sb = lambda name, shape, dt: ctx.enter_context(nc.sbuf_tensor(name, shape, dt))
        ps = lambda name, shape: ctx.enter_context(nc.psum_tensor(name, shape, F32))

        x_sb = sb("x_sb", [P, NT * D], F32)
        xnfm = sb("xnfm", [P, NT * D], BF16)
        wsb = sb("wsb", [P, NC * D], F32)
        wT = sb("wT", [P, NC * D], BF16)
        p_sb = sb("p_sb", [P, NC * D], BF16)
        n_sb = sb("n_sb", [P, NC * D], BF16)
        wqkv_sb = sb("wqkv_sb", [P, NC * 3 * H], BF16)
        wob_sb = sb("wob_sb", [H + 1, D], BF16)
        id_sb = sb("id_sb", [P, P], BF16)
        mask_sb = sb("mask_sb", [P, P], BF16)
        alpha_sb = sb("alpha_sb", [P, 1], F32)
        alpha_h = sb("alpha_h", [P, 1], F32)
        xn_sb = sb("xn_sb", [P, 2 * D], BF16)
        qkphi = sb("qkphi", [P, 4 * P], BF16)
        qkT = sb("qkT", [H, 4 * 2 * P], BF16)
        am = sb("am", [P, 4 * P], BF16)
        v1 = sb("v1", [P, 4 * (H + 1)], BF16)
        s1 = sb("s1", [H, 4 * (H + 1)], BF16)
        onb = sb("onb", [P, H], BF16)
        o1T = sb("o1T", [H + 1, NT * P], BF16)
        rn_sb = sb("rn_sb", [P, P], F32)
        e_sb = sb("e_sb", [P, 4 * P], F32)
        junk_d = sb("junk_d", [P, D], F32)
        junk_s = sb("junk_s", [P, D], BF16)
        stats = sb("stats", [P, 48], F32)
        consts = sb("consts", [P, 8], F32)
        absp = sb("absp", [P, NC], F32)
        absphl = sb("absphl", [P, 2 * NC], BF16)
        hi_f = sb("hi_f", [P, NC], F32)
        t16 = sb("t16", [1, 2 * NC], F32)
        tot1 = sb("tot1", [1, 1], F32)
        trow = sb("trow", [1, P], F32)
        draw = sb("draw", [P, 1], F32)
        dpos = sb("dpos", [P, 1], F32)
        dneg = sb("dneg", [P, 1], F32)
        ones_cb = sb("ones_cb", [P, 1], BF16)
        ones_rf = sb("ones_rf", [1, P], F32)
        dbgo_sb = sb("dbgo_sb", [P, H + 1], F32) if DEBUG else None

        pF = [ps("pF0", [P, 512]), ps("pF1", [P, 512])]
        pState = ps("pState", [P, 128])
        pM1 = ps("pM1", [P, 448])
        pMo = ps("pMo", [P, 130])
        pMt = ps("pMt", [P, 128])
        pMa = ps("pMa", [P, 128])
        pT = ps("pT", [P, 512])

        sbq = ctx.enter_context(nc.semaphore("sbq"))
        sdxs = [ctx.enter_context(nc.semaphore(f"sdx{t}")) for t in range(NT)]
        sdws = [ctx.enter_context(nc.semaphore(f"sdw{c}")) for c in range(NC)]
        sdc = ctx.enter_context(nc.semaphore("sdc"))
        sdo = ctx.enter_context(nc.semaphore("sdo"))
        ste = ctx.enter_context(nc.semaphore("ste"))
        sdve = ctx.enter_context(nc.semaphore("sdve"))
        ssc = ctx.enter_context(nc.semaphore("ssc"))
        sgp = ctx.enter_context(nc.semaphore("sgp"))

        C_NM, C_PM, C_E6, C_DS, C_NDS = range(5)
        S_SUM, S_SSQ, S_NEGM, S_T2, S_U, S_LNV, S_RINV, S_DN, S_RR, S_LN2 = range(10)
        N_CONST_DMA = 4 + NC

        def EV(eng, key):
            return trk[eng].E[key]

        def mk_waiter(e):
            lastw = {}

            def w(sem, val):
                if e is None or val is None:
                    return
                if isinstance(val, tuple):
                    val = EV(*val)
                if val > lastw.get(id(sem), -1):
                    lastw[id(sem)] = val
                    e.wait_ge(sem, val)
            return w

        def stc(b, i):
            return stats[:, 12 * b + i: 12 * b + i + 1]

        def xt(t):
            return x_sb[:, t * D:(t + 1) * D]

        def xnfm_c(t, c):
            return xnfm[:, t * D + c * P: t * D + (c + 1) * P]

        sems = {"te": ste, "dve": sdve, "sc": ssc, "gp": sgp}


        def emit_engine(which, e):
            w = mk_waiter(e)
            tk = trk[which]
            sem = sems[which]

            def fin(ins, key, inc=1):
                tk.n += inc - 1
                tk.ev(key)
                if e is not None:
                    ins.then_inc(sem, inc)

            if which == "dve":
                if e is not None:
                    e.memset(consts[:, C_NM:C_NM + 1], -1.0 / D)
                    e.memset(consts[:, C_PM:C_PM + 1], 1.0 / D)
                    e.memset(consts[:, C_E6:C_E6 + 1], 1e-6)
                    e.memset(consts[:, C_DS:C_DS + 1], 0.7 / (D * D))
                    e.memset(consts[:, C_NDS:C_NDS + 1], -0.7 / (D * D))
                    e.memset(ones_cb[:], 1.0)
                    e.memset(ones_rf[:], 1.0)
                    for _bb in range(4):
                        e.memset(v1[:, _bb * (H + 1) + H: _bb * (H + 1) + H + 1], 1.0)
                    e.memset(s1[:, 0:H + 1], 0.0)
                    ins = e.memset(o1T[H:H + 1, :], 1.0)
                fin(ins if e is not None else None, "prep")
            if which == "sc":
                if e is not None:
                    for c in range(NC):
                        e.dma_start(out=wsb[:, c * D:(c + 1) * D],
                                    in_=wf_d[c * P:(c + 1) * P, :]).then_inc(sdws[c], 16)
            if which == "gp":
                if e is not None:
                    for c in range(NC):
                        e.dma_start(out=wqkv_sb[:, c * 3 * H:(c + 1) * 3 * H],
                                    in_=wqkv_d[c * P:(c + 1) * P, :]).then_inc(sdc, 16)
                    e.dma_start(out=wob_sb[:], in_=wob_d).then_inc(sdc, 16)

            for ent in eng_sched[which]:
                st, t = ent
                b = t % 2
                b4 = t % 4
                if st == "A1":
                    if which == "dve":
                        w(sdxs[t], 16)
                        ins = None if e is None else e.tensor_scalar(
                            junk_d[:], xt(t), 0.0, 0.0, ALU.add, ALU.add,
                            accum_out=stc(b4, S_SUM))
                        fin(ins, f"sum{t}")
                    elif which == "sc":
                        w(sdxs[t], 16)
                        ins = None if e is None else e.activation(
                            junk_s[:], xt(t), AF.Square, accum_out=stc(b4, S_SSQ))
                        fin(ins, f"ssq{t}")

                elif st == "A2":
                    if which == "sc":
                        w(sdve, ("dve", f"sum{t}"))
                        if e is not None:
                            e.activation(stc(b4, S_NEGM), stc(b4, S_SUM), AF.Identity,
                                         scale=consts[:, C_NM:C_NM + 1])
                            ins = e.activation(stc(b4, S_T2), stc(b4, S_SSQ),
                                               AF.Identity, scale=consts[:, C_PM:C_PM + 1])
                        fin(ins if e is not None else None, f"nm{t}", 2)

                elif st == "A3":
                    if which == "dve":
                        w(ssc, ("sc", f"nm{t}"))
                        ins = None if e is None else e.scalar_tensor_tensor(
                            stc(b4, S_U), stc(b4, S_NEGM), stc(b4, S_NEGM),
                            stc(b4, S_T2), ALU.mult, ALU.subtract)
                        fin(ins, f"u{t}")

                elif st == "A4":
                    if which == "sc":
                        w(sdve, ("dve", f"u{t}"))
                        if e is not None:
                            e.activation(stc(b4, S_LNV), stc(b4, S_U), AF.Ln,
                                         scale=F1(-1.0), bias=F1(1e-5))
                            ins = e.activation(stc(b4, S_RINV), stc(b4, S_LNV),
                                               AF.Exp, scale=F1(-0.5))
                        fin(ins if e is not None else None, f"rinv{t}", 2)

                elif st == "A5":
                    if which == "dve":
                        w(ssc, ("sc", f"rinv{t}"))
                        if t >= 2:
                            w(ste, ("te", f"trg2_{t - 2}"))
                        ins = None if e is None else e.tensor_scalar(
                            xn_sb[:, b * D:(b + 1) * D], xt(t), stc(b4, S_NEGM),
                            stc(b4, S_RINV), ALU.add, ALU.mult)
                        fin(ins, f"xn{t}")

                elif st == "T1":
                    if which == "te":
                        w(sdve, ("dve", f"xn{t}"))
                        w(sdc, 16 * N_CONST_DMA)
                        if t >= 1:
                            w(ssc, ("sc", f"egB2_{t - 1}"))
                        for s_ in range(4):
                            if e is not None:
                                ins = e.matmul(pT[:, s_ * P:(s_ + 1) * P],
                                               xn_sb[:, b * D + s_ * P: b * D + (s_ + 1) * P],
                                               id_sb[:], start=True, stop=True)
                        fin(ins if e is not None else None, f"trg1_{t}")

                elif st == "G1":
                    if which == "dve":
                        w(ste, ("te", f"trg1_{t}"))
                        ins = None if e is None else e.tensor_copy(
                            xnfm[:, t * D: t * D + 512], pT[:, 0:512])
                        fin(ins, f"egA1_{t}")

                elif st == "T2":
                    if which == "te":
                        w(sdve, ("dve", f"egA1_{t}"))
                        for s_ in range(4):
                            if e is not None:
                                ins = e.matmul(pT[:, s_ * P:(s_ + 1) * P],
                                               xn_sb[:, b * D + (4 + s_) * P: b * D + (5 + s_) * P],
                                               id_sb[:], start=True, stop=True)
                        fin(ins if e is not None else None, f"trg2_{t}")

                elif st == "G2":
                    if which == "sc":
                        w(ste, ("te", f"trg2_{t}"))
                        ins = None if e is None else e.activation(
                            xnfm[:, t * D + 512: t * D + 1024], pT[:, 0:512],
                            AF.Identity)
                        fin(ins, f"egB2_{t}")

                elif st == "QK":
                    if which == "te":
                        w(ssc, ("sc", f"egB2_{t}"))
                        w(sdc, 16 * N_CONST_DMA)
                        if t >= 1:
                            w(ssc, ("sc", f"e{t - 1}"))
                            w(ssc, ("sc", f"v{t - 1}"))
                            w(sdve, ("dve", f"phi{t - 1}"))
                        if t >= 3:
                            w(ssc, ("sc", f"qkTe{t - 3}"))
                        for c in range(NC):
                            if e is not None:
                                ins = e.matmul(pM1[:, 0:3 * H], xnfm_c(t, c),
                                               wqkv_sb[:, c * 3 * H:(c + 1) * 3 * H],
                                               start=(c == 0), stop=(c == NC - 1))
                        fin(ins if e is not None else None, f"qkv{t}")

                elif st == "PH":
                    if which == "sc":
                        w(ste, ("te", f"qkv{t}"))
                        if t >= 2:
                            w(ste, ("te", f"qkT{t - 2}"))
                        if t >= 1:
                            w(sdve, ("dve", f"phi{t - 1}"))
                        if e is not None:
                            e.activation(rn_sb[:], pM1[:, 0:P], AF.Relu, scale=F1(-1.0))
                            ins = e.activation(e_sb[:, b4 * P:(b4 + 1) * P], rn_sb[:],
                                               AF.Exp, scale=F1(-1.0))
                        fin(ins if e is not None else None, f"e{t}", 2)
                        ins = None if e is None else e.activation(
                            v1[:, b4 * (H + 1): b4 * (H + 1) + H], pM1[:, P:P + H],
                            AF.Identity)
                        fin(ins, f"v{t}")
                    elif which == "dve":
                        w(ssc, ("sc", f"e{t}"))
                        w(ssc, ("sc", f"v{t}"))
                        ins = None if e is None else e.scalar_tensor_tensor(
                            qkphi[:, b4 * P:(b4 + 1) * P], pM1[:, 0:P], 0.0,
                            e_sb[:, b4 * P:(b4 + 1) * P], ALU.max, ALU.add)
                        fin(ins, f"phi{t}")

                elif st == "TR":
                    if which == "te":
                        w(sdve, ("dve", f"phi{t}"))
                        w(ssc, ("sc", f"v{t}"))
                        if t + 1 < NT:
                            w(ssc, ("sc", f"e{t + 1}"))
                            w(ssc, ("sc", f"v{t + 1}"))
                            w(sdve, ("dve", f"phi{t + 1}"))
                        if t >= 1:
                            w(ssc, ("sc", f"qkTe{t - 1}"))
                        if e is not None:
                            e.matmul(pM1[0:H, 192:320], qkphi[:, b4 * P: b4 * P + H],
                                     id_sb[:], start=True, stop=True)
                            ins = e.matmul(pM1[0:H, 320:448],
                                           qkphi[:, b4 * P + H:(b4 + 1) * P],
                                           id_sb[:], start=True, stop=True)
                        fin(ins if e is not None else None, f"qkT{t}")
                        w(ssc, ("sc", f"v{t}"))
                        if t >= 1:
                            w(ssc, ("sc", f"se{t}"))
                        else:
                            w(sdve, ("dve", "prep"))
                            w(ssc, ("sc", "t16"))
                        ins = None if e is None else e.matmul(
                            pState[0:H, 0:H + 1], qkphi[:, b4 * P + H:(b4 + 1) * P],
                            v1[:, b4 * (H + 1):(b4 + 1) * (H + 1)],
                            start=(t == 0), stop=(t == NT - 1))
                        fin(ins, f"state{t}")

                elif st == "SE":
                    if which == "sc":
                        w(ste, ("te", f"qkT{t}"))
                        if t + 2 < NT:
                            w(sdve, ("dve", f"phi{t + 2}"))
                        ins = None if e is None else e.activation(
                            qkT[:, b4 * 2 * P:(b4 + 1) * 2 * P], pM1[0:H, 192:448],
                            AF.Identity)
                        fin(ins, f"qkTe{t}")
                        w(ste, ("te", f"state{t}"))
                        if t + 1 < NT:
                            bn4 = (t + 1) % 4
                            ins = None if e is None else e.activation(
                                s1[:, bn4 * (H + 1):(bn4 + 1) * (H + 1)],
                                pState[0:H, 0:H + 1], AF.Identity)
                            fin(ins, f"se{t + 1}")

                elif st == "AM":
                    if which == "te":
                        w(ssc, ("sc", f"qkTe{t}"))
                        if t >= 1:
                            w(sdve, ("dve", f"am{t - 1}"))
                        ins = None if e is None else e.matmul(
                            pMa[:, 0:P], qkT[:, b4 * 2 * P + P: b4 * 2 * P + 2 * P],
                            qkT[:, b4 * 2 * P: b4 * 2 * P + P], start=True, stop=True)
                        fin(ins, f"A{t}")
                    elif which == "dve":
                        w(ste, ("te", f"A{t}"))
                        ins = None if e is None else e.tensor_tensor(
                            am[:, b4 * P:(b4 + 1) * P], pMa[:, 0:P], mask_sb[:],
                            ALU.mult)
                        fin(ins, f"am{t}")

                elif st == "OM":
                    ob = (t % 2) * 65
                    if which == "te":
                        w(sdve, ("dve", f"am{t}"))
                        if t >= 1:
                            w(ssc, ("sc", f"se{t}"))
                        else:
                            w(sdve, ("dve", "prep"))
                        if t >= 1:
                            w(sdve, ("dve", f"r{t - 1}"))
                            w(ssc, ("sc", f"on{t - 1}"))
                        if e is not None:
                            e.matmul(pMo[:, ob:ob + H + 1], am[:, b4 * P:(b4 + 1) * P],
                                     v1[:, b4 * (H + 1):(b4 + 1) * (H + 1)],
                                     start=True, stop=False)
                            ins = e.matmul(pMo[:, ob:ob + H + 1],
                                           qkT[:, b4 * 2 * P: b4 * 2 * P + P],
                                           s1[:, b4 * (H + 1):(b4 + 1) * (H + 1)],
                                           start=False, stop=True)
                        fin(ins if e is not None else None, f"o{t}")

                elif st == "ON":
                    ob = (t % 2) * 65
                    if which == "dve":
                        w(ste, ("te", f"o{t}"))
                        if t >= 1:
                            w(ssc, ("sc", f"on{t - 1}"))
                        ins = None if e is None else e.tensor_scalar(
                            stc(b4, S_DN), pMo[:, ob + H:ob + H + 1],
                            consts[:, C_E6:C_E6 + 1], None, ALU.max)
                        fin(ins, f"r{t}")
                    elif which == "sc":
                        w(sdve, ("dve", f"r{t}"))
                        if e is not None:
                            e.activation(stc(b4, S_LN2), stc(b4, S_DN), AF.Ln)
                            e.activation(stc(b4, S_RR), stc(b4, S_LN2), AF.Exp,
                                         scale=F1(-1.0))
                            ins = e.activation(onb[:], pMo[:, ob:ob + H], AF.Identity,
                                               scale=stc(b4, S_RR))
                        fin(ins if e is not None else None, f"on{t}", 3)
                    elif which == "te":
                        w(ssc, ("sc", f"on{t}"))
                        if t >= 1:
                            w(ssc, ("sc", f"o1Te{t - 1}"))
                        ins = None if e is None else e.matmul(
                            pMt[0:H, 0:P], onb[:], id_sb[:],
                            start=True, stop=True)
                        fin(ins, f"oT{t}")

                elif st == "OW":
                    if which == "sc":
                        w(ste, ("te", f"oT{t}"))
                        ins = None if e is None else e.activation(
                            o1T[0:H, t * P:(t + 1) * P], pMt[0:H, 0:P],
                            AF.Identity)
                        fin(ins, f"o1Te{t}")
                    elif which == "te":
                        w(ssc, ("sc", f"o1Te{t}"))
                        w(ssc, ("sc", f"egB2_{t}"))
                        w(sdc, 16 * N_CONST_DMA)
                        if t == 0:
                            w(sdve, ("dve", "prep"))
                            w(sdve, ("dve", f"wsub{NC - 1}"))
                        if t >= 1:
                            w(sdve, ("dve", f"ev_{t - 1}"))
                        if e is not None:
                            e.matmul(pF[0][:], o1T[:, t * P:(t + 1) * P],
                                     wob_sb[:, 0:512], start=True, stop=False)
                            e.matmul(pF[1][:], o1T[:, t * P:(t + 1) * P],
                                     wob_sb[:, 512:1024], start=True, stop=False)
                        for c in range(NC):
                            if e is not None:
                                e.matmul(pF[0][:], xnfm_c(t, c),
                                         wT[:, c * D: c * D + 512],
                                         start=False, stop=(c == NC - 1))
                                ins = e.matmul(pF[1][:], xnfm_c(t, c),
                                               wT[:, c * D + 512:(c + 1) * D],
                                               start=False, stop=(c == NC - 1))
                        fin(ins if e is not None else None, f"ffn{t}")

                elif st == "E":
                    if which == "dve":
                        w(ste, ("te", f"ffn{t}"))
                        if e is not None:
                            e.scalar_tensor_tensor(xt(t)[:, 0:512], pF[0][:], 0.0,
                                                   xt(t)[:, 0:512], ALU.bypass, ALU.add)
                            ins = e.scalar_tensor_tensor(xt(t)[:, 512:1024],
                                                         pF[1][:], 0.0,
                                                         xt(t)[:, 512:1024],
                                                         ALU.bypass, ALU.add)
                        fin(ins if e is not None else None, f"ev_{t}", 2)

                elif st == "EO":
                    if which == "gp":
                        w(sdve, ("dve", f"ev_{t}"))
                        if e is not None:
                            e.dma_start(out=out_d[t * P:(t + 1) * P, :],
                                        in_=xt(t)).then_inc(sdo, 16)

                elif st == "Wabs":
                    if t < 4:
                        if which == "dve":
                            w(sdws[t], 16)
                            ins = None if e is None else e.reduce_sum(
                                absp[:, t:t + 1], wsb[:, t * D:(t + 1) * D],
                                axis=AX.X, apply_absolute_value=True)
                            fin(ins, f"abs{t}")
                    else:
                        if which == "sc":
                            w(sdws[t], 16)
                            ins = None if e is None else e.activation(
                                junk_s[:], wsb[:, t * D:(t + 1) * D], AF.Abs,
                                accum_out=absp[:, t:t + 1])
                            fin(ins, f"abs{t}")

                elif st == "Wdelta":
                    if which == "sc":
                        w(sdve, ("dve", "abs3"))
                        if e is not None:
                            e.activation(absphl[:, 0:NC], absp[:], AF.Identity)
                            ins = e.activation(hi_f[:], absphl[:, 0:NC], AF.Identity)
                        fin(ins if e is not None else None, "dhif", 2)
                        w(ste, ("te", "dmm1"))
                        ins = None if e is None else e.activation(
                            t16[:], pState[0:1, 100:116], AF.Identity)
                        fin(ins, "t16")
                        w(sdve, ("dve", "tot1"))
                        ins = None if e is None else e.activation(
                            trow[:], ones_rf[:], AF.Identity, scale=tot1[0:1, 0:1])
                        fin(ins, "trow")
                        w(sbq, 16)
                        w(sdc, 16 * N_CONST_DMA)
                        if e is not None:
                            e.activation(dpos[:], draw[:], AF.Identity,
                                         scale=consts[:, C_DS:C_DS + 1])
                            e.activation(dneg[:], draw[:], AF.Identity,
                                         scale=consts[:, C_NDS:C_NDS + 1])
                            ins = e.activation(alpha_h[:], alpha_sb[:], AF.Identity,
                                               scale=F1(0.5))
                        fin(ins if e is not None else None, "delta", 3)
                    elif which == "gp":
                        w(ssc, ("sc", "dhif"))
                        ins = None if e is None else e.tensor_tensor(
                            absphl[:, NC:2 * NC], absp[:], hi_f[:], ALU.subtract)
                        fin(ins, "dsplit")
                        w(ssc, ("sc", "trow"))
                        if e is not None:
                            e.dma_start(out=draw[:], in_=trow[0:1, 0:P]).then_inc(sbq, 16)
                    elif which == "te":
                        w(sgp, ("gp", "dsplit"))
                        ins = None if e is None else e.matmul(
                            pState[0:1, 100:116], ones_cb[:], absphl[:],
                            start=True, stop=True)
                        fin(ins, "dmm1")
                    elif which == "dve":
                        w(ssc, ("sc", "t16"))
                        ins = None if e is None else e.reduce_sum(
                            tot1[:], t16[:], axis=AX.X)
                        fin(ins, "tot1")

                elif st == "Wtern":
                    c = t
                    if c < 4:
                        if which == "sc":
                            w(sdws[c], 16)
                            if e is not None:
                                e.activation(p_sb[:, c * D:(c + 1) * D],
                                             wsb[:, c * D:(c + 1) * D], AF.Sign,
                                             bias=dneg[:])
                                ins = e.activation(n_sb[:, c * D:(c + 1) * D],
                                                   wsb[:, c * D:(c + 1) * D], AF.Sign,
                                                   bias=dpos[:])
                            fin(ins if e is not None else None, f"pos{c}", 2)
                    else:
                        if which == "dve":
                            w(ssc, ("sc", "delta"))
                            w(sdws[c], 16)
                            if e is not None:
                                e.tensor_scalar(
                                    p_sb[:, c * D:(c + 1) * D], wsb[:, c * D:(c + 1) * D],
                                    dpos[:], alpha_sb[:], ALU.is_gt, ALU.mult)
                                ins = e.tensor_scalar(
                                    n_sb[:, c * D:(c + 1) * D], wsb[:, c * D:(c + 1) * D],
                                    dneg[:], alpha_sb[:], ALU.is_lt, ALU.mult)
                            fin(ins if e is not None else None, f"pos{c}", 2)

                elif st == "Wsub":
                    c = t
                    if which == "dve":
                        if c < 4:
                            w(ssc, ("sc", f"pos{c}"))
                            if e is not None:
                                e.scalar_tensor_tensor(
                                    n_sb[:, c * D:(c + 1) * D],
                                    p_sb[:, c * D:(c + 1) * D], 0.0,
                                    n_sb[:, c * D:(c + 1) * D], ALU.bypass, ALU.add)
                                ins = e.tensor_scalar(
                                    wT[:, c * D:(c + 1) * D],
                                    n_sb[:, c * D:(c + 1) * D],
                                    alpha_h[:], 0.0, ALU.mult, ALU.add)
                            fin(ins if e is not None else None, f"wsub{c}", 2)
                        else:
                            ins = None if e is None else e.tensor_tensor(
                                wT[:, c * D:(c + 1) * D], p_sb[:, c * D:(c + 1) * D],
                                n_sb[:, c * D:(c + 1) * D], ALU.subtract)
                            fin(ins, f"wsub{c}")

            if which == "dve" and DEBUG:
                ins = None if e is None else e.tensor_copy(dbgo_sb[:], pM2[:, 0:H + 1])
                fin(ins, "dbgo")
            if which == "gp":
                n_out = NT
                if DEBUG:
                    w(sdve, ("dve", "dbgo"))
                    if e is not None:
                        for name, src in [("dbg_wT", wT), ("dbg_xnfm", xnfm),
                                          ("dbg_o1T", o1T), ("dbg_draw", draw),
                                          ("dbg_stats", stats)]:
                            e.dma_start(out=dbg_d[name], in_=src[:]).then_inc(sdo, 16)
                    n_out += 5
                if e is not None:
                    e.wait_ge(sdo, 16 * n_out)

        def emit_sync(e):
            if e is None:
                return
            for t_ in range(2):
                e.dma_start(out=xt(t_), in_=x_d[t_ * P:(t_ + 1) * P, :]).then_inc(sdxs[t_], 16)
            e.dma_start(out=id_sb[:], in_=id_d).then_inc(sdc, 16)
            e.dma_start(out=mask_sb[:], in_=mask_d).then_inc(sdc, 16)
            e.dma_start(out=alpha_sb[:], in_=alpha_d).then_inc(sdc, 16)
            for t_ in range(2, NT):
                e.dma_start(out=xt(t_), in_=x_d[t_ * P:(t_ + 1) * P, :]).then_inc(sdxs[t_], 16)

        for tkk in trk.values():
            tkk.reset(planning=True)
        for eng in ("gp", "dve", "sc", "te"):
            emit_engine(eng, None)

        for tkk in trk.values():
            tkk.reset(planning=False)
        block = ctx.enter_context(nc.Block())

        @block.sync
        def _(sync):
            emit_sync(sync)

        @block.gpsimd
        def _(gp):
            emit_engine("gp", gp)

        @block.vector
        def _(ve):
            emit_engine("dve", ve)

        @block.scalar
        def _(sc):
            emit_engine("sc", sc)

        @block.tensor
        def _(te):
            emit_engine("te", te)

    return nc


_NC_CACHE = None


def kernel(L0_attn_x, L0_attn_W_q, L0_attn_W_k, L0_attn_W_v, L0_attn_W_o,
           L0_ffn_weight, L0_ffn_alpha, L0_ffn_bias):
    global _NC_CACHE
    x = np.ascontiguousarray(np.asarray(L0_attn_x, dtype=np.float32))
    wq = np.asarray(L0_attn_W_q, dtype=np.float32)
    wk = np.asarray(L0_attn_W_k, dtype=np.float32)
    wv = np.asarray(L0_attn_W_v, dtype=np.float32)
    wo = np.asarray(L0_attn_W_o, dtype=np.float32)
    wf = np.ascontiguousarray(np.asarray(L0_ffn_weight, dtype=np.float32).T)
    alpha = np.asarray(L0_ffn_alpha, dtype=np.float32)
    bias = np.asarray(L0_ffn_bias, dtype=np.float32)

    wqkv = np.ascontiguousarray(np.concatenate([wq, wk, wv], axis=1))
    wob = np.ascontiguousarray(np.concatenate([wo, bias[None, :]], axis=0))
    alpha_rep = np.ascontiguousarray(
        np.broadcast_to(alpha.reshape(1, 1), (P, 1)).astype(np.float32))

    ident = np.eye(P, dtype=np.float32).astype(ml_dtypes.bfloat16)
    maskt = np.triu(np.ones((P, P), dtype=np.float32)).astype(ml_dtypes.bfloat16)

    if _NC_CACHE is None:
        _NC_CACHE = build_nc()
    nc = _NC_CACHE

    in_maps = []
    for i in range(B):
        in_maps.append({
            "x": np.ascontiguousarray(x[i]),
            "wf": wf,
            "wqkv": wqkv,
            "wob": wob,
            "alpha": alpha_rep,
            "ident": ident,
            "maskt": maskt,
        })
    res = run_bass_kernel_spmd(nc, in_maps, core_ids=list(range(B)))
    out = np.stack([np.asarray(res.results[i]["out"]) for i in range(B)], axis=0)
    return out.astype(np.float32)


# version 19
# speedup vs baseline: 1.0018x; 1.0018x over previous
import numpy as np
import ml_dtypes
from contextlib import ExitStack

import concourse.bass as bass
import concourse.mybir as mybir
from concourse.bass_utils import run_bass_kernel_spmd

F32 = mybir.dt.float32
BF16 = mybir.dt.bfloat16
AF = mybir.ActivationFunctionType
ALU = mybir.AluOpType
AX = mybir.AxisListType

P = 128
D = 1024
H = 64
T = 2048
NT = T // P
NC = D // P
B = 8

F1 = np.float32

DEBUG = False

OFF = {"A1": 0, "A2": 1, "A3": 2, "A4": 3, "A5": 4,
       "T1": 5, "G1": 5, "T2": 5, "G2": 5,
       "QK": 6, "PH": 7, "TR": 8, "SE": 9, "AM": 9,
       "OM": 10, "ON": 11, "OW": 12, "E": 13, "EO": 14}


class Trk:

    def __init__(self):
        self.E = {}
        self.planning = True
        self.n = 0

    def reset(self, planning):
        self.planning = planning
        self.n = 0

    def ev(self, key):
        self.n += 1
        if self.planning:
            self.E[key] = self.n
        else:
            assert self.E[key] == self.n, (key, self.E[key], self.n)
        return self.n


def make_sched():
    wabs_at = {2: [0, 1, 2], 3: [3, 4, 5], 4: [6, 7]}
    wtern_at = {6: [0, 4], 7: [1, 5], 8: [2, 6], 9: [3, 7]}
    wsub_at = {8: [0, 4], 9: [1, 5], 10: [2, 6], 11: [3, 7]}
    iters = [[] for _ in range(NT + 17)]
    for t in range(NT):
        for s, o in OFF.items():
            iters[t + o].append((s, t))
    for i, cs in wabs_at.items():
        for c in cs:
            iters[i].append(("Wabs", c))
    iters[5].append(("Wdelta", 0))
    for i, cs in wtern_at.items():
        for c in cs:
            iters[i].append(("Wtern", c))
    for i, cs in wsub_at.items():
        for c in cs:
            iters[i].append(("Wsub", c))
    return iters


def build_nc():
    nc = bass.Bass("TRN2", target_bir_lowering=False)
    dbg_d = {}
    if DEBUG:
        dbg_d["dbg_wT"] = nc.dram_tensor("dbg_wT", [P, NC * D], BF16, kind="ExternalOutput").ap()
        dbg_d["dbg_xnfm"] = nc.dram_tensor("dbg_xnfm", [P, NT * D], BF16, kind="ExternalOutput").ap()
        dbg_d["dbg_o1T"] = nc.dram_tensor("dbg_o1T", [H + 1, NT * P], BF16, kind="ExternalOutput").ap()
        dbg_d["dbg_draw"] = nc.dram_tensor("dbg_draw", [P, 1], F32, kind="ExternalOutput").ap()
        dbg_d["dbg_stats"] = nc.dram_tensor("dbg_stats", [P, 48], F32, kind="ExternalOutput").ap()

    x_d = nc.dram_tensor("x", [T, D], F32, kind="ExternalInput").ap()
    wf_d = nc.dram_tensor("wf", [D, D], F32, kind="ExternalInput").ap()
    wqkv_d = nc.dram_tensor("wqkv", [D, 3 * H], F32, kind="ExternalInput").ap()
    wob_d = nc.dram_tensor("wob", [H + 1, D], F32, kind="ExternalInput").ap()
    alpha_d = nc.dram_tensor("alpha", [P, 1], F32, kind="ExternalInput").ap()
    id_d = nc.dram_tensor("ident", [P, P], BF16, kind="ExternalInput").ap()
    mask_d = nc.dram_tensor("maskt", [P, P], BF16, kind="ExternalInput").ap()
    out_d = nc.dram_tensor("out", [T, D], F32, kind="ExternalOutput").ap()

    trk = {"te": Trk(), "dve": Trk(), "sc": Trk(), "gp": Trk()}
    iters = make_sched()
    PRIO = {
        "te": {"T1": 1, "QK": 2, "TR": 3, "AM": 4, "OM": 5, "ON": 6,
               "T2": 7, "Wdelta": 8, "OW": 9},
        "dve": {"ON": 0, "PH": 1, "G1": 2, "E": 3, "AM": 4,
                "A5": 5, "A3": 6, "A1": 7,
                "Wabs": 9, "Wtern": 10, "Wsub": 11, "Wdelta": 12},
        "sc": {"PH": 0, "SE": 1, "ON": 2, "OW": 3, "A4": 4, "A2": 5,
               "A1": 6, "G2": 7, "Wtern": 8, "Wdelta": 9, "Wabs": 10},
        "gp": {"Wdelta": 0, "EO": 1},
    }
    eng_sched = {}
    for eng, pr in PRIO.items():
        seq = []
        for it in iters:
            seq.extend(sorted((ent for ent in it if ent[0] in pr),
                              key=lambda ent: pr[ent[0]]))
        eng_sched[eng] = seq

    with ExitStack() as ctx:
        sb = lambda name, shape, dt: ctx.enter_context(nc.sbuf_tensor(name, shape, dt))
        ps = lambda name, shape: ctx.enter_context(nc.psum_tensor(name, shape, F32))

        x_sb = sb("x_sb", [P, NT * D], F32)
        xnfm = sb("xnfm", [P, NT * D], BF16)
        wsb = sb("wsb", [P, NC * D], F32)
        wT = sb("wT", [P, NC * D], BF16)
        p_sb = sb("p_sb", [P, NC * D], BF16)
        n_sb = sb("n_sb", [P, NC * D], BF16)
        wqkv_sb = sb("wqkv_sb", [P, NC * 3 * H], BF16)
        wob_sb = sb("wob_sb", [H + 1, D], BF16)
        id_sb = sb("id_sb", [P, P], BF16)
        mask_sb = sb("mask_sb", [P, P], BF16)
        alpha_sb = sb("alpha_sb", [P, 1], F32)
        alpha_h = sb("alpha_h", [P, 1], F32)
        xn_sb = sb("xn_sb", [P, 2 * D], BF16)
        qkphi = sb("qkphi", [P, 4 * P], BF16)
        qkT = sb("qkT", [H, 4 * 2 * P], BF16)
        am = sb("am", [P, 4 * P], BF16)
        v1 = sb("v1", [P, 4 * (H + 1)], BF16)
        s1 = sb("s1", [H, 4 * (H + 1)], BF16)
        onb = sb("onb", [P, H], BF16)
        o1T = sb("o1T", [H + 1, NT * P], BF16)
        rn_sb = sb("rn_sb", [P, P], F32)
        e_sb = sb("e_sb", [P, 4 * P], F32)
        junk_d = sb("junk_d", [P, D], F32)
        junk_s = sb("junk_s", [P, D], BF16)
        stats = sb("stats", [P, 48], F32)
        consts = sb("consts", [P, 8], F32)
        absp = sb("absp", [P, NC], F32)
        absphl = sb("absphl", [P, 2 * NC], BF16)
        hi_f = sb("hi_f", [P, NC], F32)
        t16 = sb("t16", [1, 2 * NC], F32)
        tot1 = sb("tot1", [1, 1], F32)
        trow = sb("trow", [1, P], F32)
        draw = sb("draw", [P, 1], F32)
        dpos = sb("dpos", [P, 1], F32)
        dneg = sb("dneg", [P, 1], F32)
        ones_cb = sb("ones_cb", [P, 1], BF16)
        ones_rf = sb("ones_rf", [1, P], F32)
        dbgo_sb = sb("dbgo_sb", [P, H + 1], F32) if DEBUG else None

        pF = [ps("pF0", [P, 512]), ps("pF1", [P, 512])]
        pState = ps("pState", [P, 128])
        pM1 = ps("pM1", [P, 448])
        pMo = ps("pMo", [P, 130])
        pMt = ps("pMt", [P, 128])
        pMa = ps("pMa", [P, 128])
        pT = ps("pT", [P, 512])

        sbq = ctx.enter_context(nc.semaphore("sbq"))
        sdxs = [ctx.enter_context(nc.semaphore(f"sdx{t}")) for t in range(NT)]
        sdws = [ctx.enter_context(nc.semaphore(f"sdw{c}")) for c in range(NC)]
        sdc = ctx.enter_context(nc.semaphore("sdc"))
        sdo = ctx.enter_context(nc.semaphore("sdo"))
        ste = ctx.enter_context(nc.semaphore("ste"))
        sdve = ctx.enter_context(nc.semaphore("sdve"))
        ssc = ctx.enter_context(nc.semaphore("ssc"))
        sgp = ctx.enter_context(nc.semaphore("sgp"))

        C_NM, C_PM, C_E6, C_DS, C_NDS = range(5)
        S_SUM, S_SSQ, S_NEGM, S_T2, S_U, S_LNV, S_RINV, S_DN, S_RR, S_LN2 = range(10)
        N_CONST_DMA = 4 + NC

        def EV(eng, key):
            return trk[eng].E[key]

        def mk_waiter(e):
            lastw = {}

            def w(sem, val):
                if e is None or val is None:
                    return
                if isinstance(val, tuple):
                    val = EV(*val)
                if val > lastw.get(id(sem), -1):
                    lastw[id(sem)] = val
                    e.wait_ge(sem, val)
            return w

        def stc(b, i):
            return stats[:, 12 * b + i: 12 * b + i + 1]

        def xt(t):
            return x_sb[:, t * D:(t + 1) * D]

        def xnfm_c(t, c):
            return xnfm[:, t * D + c * P: t * D + (c + 1) * P]

        sems = {"te": ste, "dve": sdve, "sc": ssc, "gp": sgp}


        def emit_engine(which, e):
            w = mk_waiter(e)
            tk = trk[which]
            sem = sems[which]

            def fin(ins, key, inc=1):
                tk.n += inc - 1
                tk.ev(key)
                if e is not None:
                    ins.then_inc(sem, inc)

            if which == "dve":
                if e is not None:
                    e.memset(consts[:, C_NM:C_NM + 1], -1.0 / D)
                    e.memset(consts[:, C_PM:C_PM + 1], 1.0 / D)
                    e.memset(consts[:, C_E6:C_E6 + 1], 1e-6)
                    e.memset(consts[:, C_DS:C_DS + 1], 0.7 / (D * D))
                    e.memset(consts[:, C_NDS:C_NDS + 1], -0.7 / (D * D))
                    e.memset(ones_cb[:], 1.0)
                    e.memset(ones_rf[:], 1.0)
                    for _bb in range(4):
                        e.memset(v1[:, _bb * (H + 1) + H: _bb * (H + 1) + H + 1], 1.0)
                    e.memset(s1[:, 0:H + 1], 0.0)
                    ins = e.memset(o1T[H:H + 1, :], 1.0)
                fin(ins if e is not None else None, "prep")
            if which == "gp":
                if e is not None:
                    for c in range(NC):
                        e.dma_start(out=wqkv_sb[:, c * 3 * H:(c + 1) * 3 * H],
                                    in_=wqkv_d[c * P:(c + 1) * P, :]).then_inc(sdc, 16)
                    e.dma_start(out=wob_sb[:], in_=wob_d).then_inc(sdc, 16)

            for ent in eng_sched[which]:
                st, t = ent
                b = t % 2
                b4 = t % 4
                if st == "A1":
                    if which == "dve":
                        w(sdxs[t], 16)
                        ins = None if e is None else e.tensor_scalar(
                            junk_d[:], xt(t), 0.0, 0.0, ALU.add, ALU.add,
                            accum_out=stc(b4, S_SUM))
                        fin(ins, f"sum{t}")
                    elif which == "sc":
                        w(sdxs[t], 16)
                        ins = None if e is None else e.activation(
                            junk_s[:], xt(t), AF.Square, accum_out=stc(b4, S_SSQ))
                        fin(ins, f"ssq{t}")

                elif st == "A2":
                    if which == "sc":
                        w(sdve, ("dve", f"sum{t}"))
                        if e is not None:
                            e.activation(stc(b4, S_NEGM), stc(b4, S_SUM), AF.Identity,
                                         scale=consts[:, C_NM:C_NM + 1])
                            ins = e.activation(stc(b4, S_T2), stc(b4, S_SSQ),
                                               AF.Identity, scale=consts[:, C_PM:C_PM + 1])
                        fin(ins if e is not None else None, f"nm{t}", 2)

                elif st == "A3":
                    if which == "dve":
                        w(ssc, ("sc", f"nm{t}"))
                        ins = None if e is None else e.scalar_tensor_tensor(
                            stc(b4, S_U), stc(b4, S_NEGM), stc(b4, S_NEGM),
                            stc(b4, S_T2), ALU.mult, ALU.subtract)
                        fin(ins, f"u{t}")

                elif st == "A4":
                    if which == "sc":
                        w(sdve, ("dve", f"u{t}"))
                        if e is not None:
                            e.activation(stc(b4, S_LNV), stc(b4, S_U), AF.Ln,
                                         scale=F1(-1.0), bias=F1(1e-5))
                            ins = e.activation(stc(b4, S_RINV), stc(b4, S_LNV),
                                               AF.Exp, scale=F1(-0.5))
                        fin(ins if e is not None else None, f"rinv{t}", 2)

                elif st == "A5":
                    if which == "dve":
                        w(ssc, ("sc", f"rinv{t}"))
                        if t >= 2:
                            w(ste, ("te", f"trg2_{t - 2}"))
                        ins = None if e is None else e.tensor_scalar(
                            xn_sb[:, b * D:(b + 1) * D], xt(t), stc(b4, S_NEGM),
                            stc(b4, S_RINV), ALU.add, ALU.mult)
                        fin(ins, f"xn{t}")

                elif st == "T1":
                    if which == "te":
                        w(sdve, ("dve", f"xn{t}"))
                        w(sdc, 16 * N_CONST_DMA)
                        if t >= 1:
                            w(ssc, ("sc", f"egB2_{t - 1}"))
                        for s_ in range(4):
                            if e is not None:
                                ins = e.matmul(pT[:, s_ * P:(s_ + 1) * P],
                                               xn_sb[:, b * D + s_ * P: b * D + (s_ + 1) * P],
                                               id_sb[:], start=True, stop=True)
                        fin(ins if e is not None else None, f"trg1_{t}")

                elif st == "G1":
                    if which == "dve":
                        w(ste, ("te", f"trg1_{t}"))
                        ins = None if e is None else e.tensor_copy(
                            xnfm[:, t * D: t * D + 512], pT[:, 0:512])
                        fin(ins, f"egA1_{t}")

                elif st == "T2":
                    if which == "te":
                        w(sdve, ("dve", f"egA1_{t}"))
                        for s_ in range(4):
                            if e is not None:
                                ins = e.matmul(pT[:, s_ * P:(s_ + 1) * P],
                                               xn_sb[:, b * D + (4 + s_) * P: b * D + (5 + s_) * P],
                                               id_sb[:], start=True, stop=True)
                        fin(ins if e is not None else None, f"trg2_{t}")

                elif st == "G2":
                    if which == "sc":
                        w(ste, ("te", f"trg2_{t}"))
                        ins = None if e is None else e.activation(
                            xnfm[:, t * D + 512: t * D + 1024], pT[:, 0:512],
                            AF.Identity)
                        fin(ins, f"egB2_{t}")

                elif st == "QK":
                    if which == "te":
                        w(ssc, ("sc", f"egB2_{t}"))
                        w(sdc, 16 * N_CONST_DMA)
                        if t >= 1:
                            w(ssc, ("sc", f"e{t - 1}"))
                            w(ssc, ("sc", f"v{t - 1}"))
                            w(sdve, ("dve", f"phi{t - 1}"))
                        if t >= 3:
                            w(ssc, ("sc", f"qkTe{t - 3}"))
                        for c in range(NC):
                            if e is not None:
                                ins = e.matmul(pM1[:, 0:3 * H], xnfm_c(t, c),
                                               wqkv_sb[:, c * 3 * H:(c + 1) * 3 * H],
                                               start=(c == 0), stop=(c == NC - 1))
                        fin(ins if e is not None else None, f"qkv{t}")

                elif st == "PH":
                    if which == "sc":
                        w(ste, ("te", f"qkv{t}"))
                        if t >= 2:
                            w(ste, ("te", f"qkT{t - 2}"))
                        if t >= 1:
                            w(sdve, ("dve", f"phi{t - 1}"))
                        if e is not None:
                            e.activation(rn_sb[:], pM1[:, 0:P], AF.Relu, scale=F1(-1.0))
                            ins = e.activation(e_sb[:, b4 * P:(b4 + 1) * P], rn_sb[:],
                                               AF.Exp, scale=F1(-1.0))
                        fin(ins if e is not None else None, f"e{t}", 2)
                        ins = None if e is None else e.activation(
                            v1[:, b4 * (H + 1): b4 * (H + 1) + H], pM1[:, P:P + H],
                            AF.Identity)
                        fin(ins, f"v{t}")
                    elif which == "dve":
                        w(ssc, ("sc", f"e{t}"))
                        w(ssc, ("sc", f"v{t}"))
                        ins = None if e is None else e.scalar_tensor_tensor(
                            qkphi[:, b4 * P:(b4 + 1) * P], pM1[:, 0:P], 0.0,
                            e_sb[:, b4 * P:(b4 + 1) * P], ALU.max, ALU.add)
                        fin(ins, f"phi{t}")

                elif st == "TR":
                    if which == "te":
                        w(sdve, ("dve", f"phi{t}"))
                        w(ssc, ("sc", f"v{t}"))
                        if t + 1 < NT:
                            w(ssc, ("sc", f"e{t + 1}"))
                            w(ssc, ("sc", f"v{t + 1}"))
                            w(sdve, ("dve", f"phi{t + 1}"))
                        if t >= 1:
                            w(ssc, ("sc", f"qkTe{t - 1}"))
                        if e is not None:
                            e.matmul(pM1[0:H, 192:320], qkphi[:, b4 * P: b4 * P + H],
                                     id_sb[:], start=True, stop=True)
                            ins = e.matmul(pM1[0:H, 320:448],
                                           qkphi[:, b4 * P + H:(b4 + 1) * P],
                                           id_sb[:], start=True, stop=True)
                        fin(ins if e is not None else None, f"qkT{t}")
                        w(ssc, ("sc", f"v{t}"))
                        if t >= 1:
                            w(ssc, ("sc", f"se{t}"))
                        else:
                            w(sdve, ("dve", "prep"))
                            w(ssc, ("sc", "t16"))
                        ins = None if e is None else e.matmul(
                            pState[0:H, 0:H + 1], qkphi[:, b4 * P + H:(b4 + 1) * P],
                            v1[:, b4 * (H + 1):(b4 + 1) * (H + 1)],
                            start=(t == 0), stop=(t == NT - 1))
                        fin(ins, f"state{t}")

                elif st == "SE":
                    if which == "sc":
                        w(ste, ("te", f"qkT{t}"))
                        if t + 2 < NT:
                            w(sdve, ("dve", f"phi{t + 2}"))
                        ins = None if e is None else e.activation(
                            qkT[:, b4 * 2 * P:(b4 + 1) * 2 * P], pM1[0:H, 192:448],
                            AF.Identity)
                        fin(ins, f"qkTe{t}")
                        w(ste, ("te", f"state{t}"))
                        if t + 1 < NT:
                            bn4 = (t + 1) % 4
                            ins = None if e is None else e.activation(
                                s1[:, bn4 * (H + 1):(bn4 + 1) * (H + 1)],
                                pState[0:H, 0:H + 1], AF.Identity)
                            fin(ins, f"se{t + 1}")

                elif st == "AM":
                    if which == "te":
                        w(ssc, ("sc", f"qkTe{t}"))
                        if t >= 1:
                            w(sdve, ("dve", f"am{t - 1}"))
                        ins = None if e is None else e.matmul(
                            pMa[:, 0:P], qkT[:, b4 * 2 * P + P: b4 * 2 * P + 2 * P],
                            qkT[:, b4 * 2 * P: b4 * 2 * P + P], start=True, stop=True)
                        fin(ins, f"A{t}")
                    elif which == "dve":
                        w(ste, ("te", f"A{t}"))
                        ins = None if e is None else e.tensor_tensor(
                            am[:, b4 * P:(b4 + 1) * P], pMa[:, 0:P], mask_sb[:],
                            ALU.mult)
                        fin(ins, f"am{t}")

                elif st == "OM":
                    ob = (t % 2) * 65
                    if which == "te":
                        w(sdve, ("dve", f"am{t}"))
                        if t >= 1:
                            w(ssc, ("sc", f"se{t}"))
                        else:
                            w(sdve, ("dve", "prep"))
                        if t >= 1:
                            w(sdve, ("dve", f"r{t - 1}"))
                            w(ssc, ("sc", f"on{t - 1}"))
                        if e is not None:
                            e.matmul(pMo[:, ob:ob + H + 1], am[:, b4 * P:(b4 + 1) * P],
                                     v1[:, b4 * (H + 1):(b4 + 1) * (H + 1)],
                                     start=True, stop=False)
                            ins = e.matmul(pMo[:, ob:ob + H + 1],
                                           qkT[:, b4 * 2 * P: b4 * 2 * P + P],
                                           s1[:, b4 * (H + 1):(b4 + 1) * (H + 1)],
                                           start=False, stop=True)
                        fin(ins if e is not None else None, f"o{t}")

                elif st == "ON":
                    ob = (t % 2) * 65
                    if which == "dve":
                        w(ste, ("te", f"o{t}"))
                        if t >= 1:
                            w(ssc, ("sc", f"on{t - 1}"))
                        ins = None if e is None else e.tensor_scalar(
                            stc(b4, S_DN), pMo[:, ob + H:ob + H + 1],
                            consts[:, C_E6:C_E6 + 1], None, ALU.max)
                        fin(ins, f"r{t}")
                    elif which == "sc":
                        w(sdve, ("dve", f"r{t}"))
                        if e is not None:
                            e.activation(stc(b4, S_LN2), stc(b4, S_DN), AF.Ln)
                            e.activation(stc(b4, S_RR), stc(b4, S_LN2), AF.Exp,
                                         scale=F1(-1.0))
                            ins = e.activation(onb[:], pMo[:, ob:ob + H], AF.Identity,
                                               scale=stc(b4, S_RR))
                        fin(ins if e is not None else None, f"on{t}", 3)
                    elif which == "te":
                        w(ssc, ("sc", f"on{t}"))
                        if t >= 1:
                            w(ssc, ("sc", f"o1Te{t - 1}"))
                        ins = None if e is None else e.matmul(
                            pMt[0:H, 0:P], onb[:], id_sb[:],
                            start=True, stop=True)
                        fin(ins, f"oT{t}")

                elif st == "OW":
                    if which == "sc":
                        w(ste, ("te", f"oT{t}"))
                        ins = None if e is None else e.activation(
                            o1T[0:H, t * P:(t + 1) * P], pMt[0:H, 0:P],
                            AF.Identity)
                        fin(ins, f"o1Te{t}")
                    elif which == "te":
                        w(ssc, ("sc", f"o1Te{t}"))
                        w(ssc, ("sc", f"egB2_{t}"))
                        w(sdc, 16 * N_CONST_DMA)
                        if t == 0:
                            w(sdve, ("dve", "prep"))
                            w(sdve, ("dve", f"wsub{NC - 1}"))
                        if t >= 1:
                            w(sdve, ("dve", f"ev_{t - 1}"))
                        if e is not None:
                            e.matmul(pF[0][:], o1T[:, t * P:(t + 1) * P],
                                     wob_sb[:, 0:512], start=True, stop=False)
                            e.matmul(pF[1][:], o1T[:, t * P:(t + 1) * P],
                                     wob_sb[:, 512:1024], start=True, stop=False)
                        for c in range(NC):
                            if e is not None:
                                e.matmul(pF[0][:], xnfm_c(t, c),
                                         wT[:, c * D: c * D + 512],
                                         start=False, stop=(c == NC - 1))
                                ins = e.matmul(pF[1][:], xnfm_c(t, c),
                                               wT[:, c * D + 512:(c + 1) * D],
                                               start=False, stop=(c == NC - 1))
                        fin(ins if e is not None else None, f"ffn{t}")

                elif st == "E":
                    if which == "dve":
                        w(ste, ("te", f"ffn{t}"))
                        if e is not None:
                            e.scalar_tensor_tensor(xt(t)[:, 0:512], pF[0][:], 0.0,
                                                   xt(t)[:, 0:512], ALU.bypass, ALU.add)
                            ins = e.scalar_tensor_tensor(xt(t)[:, 512:1024],
                                                         pF[1][:], 0.0,
                                                         xt(t)[:, 512:1024],
                                                         ALU.bypass, ALU.add)
                        fin(ins if e is not None else None, f"ev_{t}", 2)

                elif st == "EO":
                    if which == "gp":
                        w(sdve, ("dve", f"ev_{t}"))
                        if e is not None:
                            e.dma_start(out=out_d[t * P:(t + 1) * P, :],
                                        in_=xt(t)).then_inc(sdo, 16)

                elif st == "Wabs":
                    if t < 4:
                        if which == "dve":
                            w(sdws[t], 16)
                            ins = None if e is None else e.reduce_sum(
                                absp[:, t:t + 1], wsb[:, t * D:(t + 1) * D],
                                axis=AX.X, apply_absolute_value=True)
                            fin(ins, f"abs{t}")
                    else:
                        if which == "sc":
                            w(sdws[t], 16)
                            ins = None if e is None else e.activation(
                                junk_s[:], wsb[:, t * D:(t + 1) * D], AF.Abs,
                                accum_out=absp[:, t:t + 1])
                            fin(ins, f"abs{t}")

                elif st == "Wdelta":
                    if which == "sc":
                        w(sdve, ("dve", "abs3"))
                        if e is not None:
                            e.activation(absphl[:, 0:NC], absp[:], AF.Identity)
                            ins = e.activation(hi_f[:], absphl[:, 0:NC], AF.Identity)
                        fin(ins if e is not None else None, "dhif", 2)
                        w(ste, ("te", "dmm1"))
                        ins = None if e is None else e.activation(
                            t16[:], pState[0:1, 100:116], AF.Identity)
                        fin(ins, "t16")
                        w(sdve, ("dve", "tot1"))
                        ins = None if e is None else e.activation(
                            trow[:], ones_rf[:], AF.Identity, scale=tot1[0:1, 0:1])
                        fin(ins, "trow")
                        w(sbq, 16)
                        w(sdc, 16 * N_CONST_DMA)
                        if e is not None:
                            e.activation(dpos[:], draw[:], AF.Identity,
                                         scale=consts[:, C_DS:C_DS + 1])
                            e.activation(dneg[:], draw[:], AF.Identity,
                                         scale=consts[:, C_NDS:C_NDS + 1])
                            ins = e.activation(alpha_h[:], alpha_sb[:], AF.Identity,
                                               scale=F1(0.5))
                        fin(ins if e is not None else None, "delta", 3)
                    elif which == "gp":
                        w(ssc, ("sc", "dhif"))
                        ins = None if e is None else e.tensor_tensor(
                            absphl[:, NC:2 * NC], absp[:], hi_f[:], ALU.subtract)
                        fin(ins, "dsplit")
                        w(ssc, ("sc", "trow"))
                        if e is not None:
                            e.dma_start(out=draw[:], in_=trow[0:1, 0:P]).then_inc(sbq, 16)
                    elif which == "te":
                        w(sgp, ("gp", "dsplit"))
                        ins = None if e is None else e.matmul(
                            pState[0:1, 100:116], ones_cb[:], absphl[:],
                            start=True, stop=True)
                        fin(ins, "dmm1")
                    elif which == "dve":
                        w(ssc, ("sc", "t16"))
                        ins = None if e is None else e.reduce_sum(
                            tot1[:], t16[:], axis=AX.X)
                        fin(ins, "tot1")

                elif st == "Wtern":
                    c = t
                    if c < 4:
                        if which == "sc":
                            w(sdws[c], 16)
                            if e is not None:
                                e.activation(p_sb[:, c * D:(c + 1) * D],
                                             wsb[:, c * D:(c + 1) * D], AF.Sign,
                                             bias=dneg[:])
                                ins = e.activation(n_sb[:, c * D:(c + 1) * D],
                                                   wsb[:, c * D:(c + 1) * D], AF.Sign,
                                                   bias=dpos[:])
                            fin(ins if e is not None else None, f"pos{c}", 2)
                    else:
                        if which == "dve":
                            w(ssc, ("sc", "delta"))
                            w(sdws[c], 16)
                            if e is not None:
                                e.tensor_scalar(
                                    p_sb[:, c * D:(c + 1) * D], wsb[:, c * D:(c + 1) * D],
                                    dpos[:], alpha_sb[:], ALU.is_gt, ALU.mult)
                                ins = e.tensor_scalar(
                                    n_sb[:, c * D:(c + 1) * D], wsb[:, c * D:(c + 1) * D],
                                    dneg[:], alpha_sb[:], ALU.is_lt, ALU.mult)
                            fin(ins if e is not None else None, f"pos{c}", 2)

                elif st == "Wsub":
                    c = t
                    if which == "dve":
                        if c < 4:
                            w(ssc, ("sc", f"pos{c}"))
                            if e is not None:
                                e.scalar_tensor_tensor(
                                    n_sb[:, c * D:(c + 1) * D],
                                    p_sb[:, c * D:(c + 1) * D], 0.0,
                                    n_sb[:, c * D:(c + 1) * D], ALU.bypass, ALU.add)
                                ins = e.tensor_scalar(
                                    wT[:, c * D:(c + 1) * D],
                                    n_sb[:, c * D:(c + 1) * D],
                                    alpha_h[:], 0.0, ALU.mult, ALU.add)
                            fin(ins if e is not None else None, f"wsub{c}", 2)
                        else:
                            ins = None if e is None else e.tensor_tensor(
                                wT[:, c * D:(c + 1) * D], p_sb[:, c * D:(c + 1) * D],
                                n_sb[:, c * D:(c + 1) * D], ALU.subtract)
                            fin(ins, f"wsub{c}")

            if which == "dve" and DEBUG:
                ins = None if e is None else e.tensor_copy(dbgo_sb[:], pM2[:, 0:H + 1])
                fin(ins, "dbgo")
            if which == "gp":
                n_out = NT
                if DEBUG:
                    w(sdve, ("dve", "dbgo"))
                    if e is not None:
                        for name, src in [("dbg_wT", wT), ("dbg_xnfm", xnfm),
                                          ("dbg_o1T", o1T), ("dbg_draw", draw),
                                          ("dbg_stats", stats)]:
                            e.dma_start(out=dbg_d[name], in_=src[:]).then_inc(sdo, 16)
                    n_out += 5
                if e is not None:
                    e.wait_ge(sdo, 16 * n_out)

        def emit_sync(e):
            if e is None:
                return
            def dx(t_):
                e.dma_start(out=xt(t_), in_=x_d[t_ * P:(t_ + 1) * P, :]).then_inc(sdxs[t_], 16)

            def dw(c):
                e.dma_start(out=wsb[:, c * D:(c + 1) * D],
                            in_=wf_d[c * P:(c + 1) * P, :]).then_inc(sdws[c], 16)

            dx(0)
            dx(1)
            dx(2)
            e.dma_start(out=id_sb[:], in_=id_d).then_inc(sdc, 16)
            e.dma_start(out=mask_sb[:], in_=mask_d).then_inc(sdc, 16)
            e.dma_start(out=alpha_sb[:], in_=alpha_d).then_inc(sdc, 16)
            for t_ in (3, 4, 5):
                dx(t_)
            for c in (0, 1, 2, 3):
                dw(c)
            dx(6)
            for c in (4, 5, 6, 7):
                dw(c)
            for t_ in range(7, NT):
                dx(t_)

        for tkk in trk.values():
            tkk.reset(planning=True)
        for eng in ("gp", "dve", "sc", "te"):
            emit_engine(eng, None)

        for tkk in trk.values():
            tkk.reset(planning=False)
        block = ctx.enter_context(nc.Block())

        @block.sync
        def _(sync):
            emit_sync(sync)

        @block.gpsimd
        def _(gp):
            emit_engine("gp", gp)

        @block.vector
        def _(ve):
            emit_engine("dve", ve)

        @block.scalar
        def _(sc):
            emit_engine("sc", sc)

        @block.tensor
        def _(te):
            emit_engine("te", te)

    return nc


_NC_CACHE = None


def kernel(L0_attn_x, L0_attn_W_q, L0_attn_W_k, L0_attn_W_v, L0_attn_W_o,
           L0_ffn_weight, L0_ffn_alpha, L0_ffn_bias):
    global _NC_CACHE
    x = np.ascontiguousarray(np.asarray(L0_attn_x, dtype=np.float32))
    wq = np.asarray(L0_attn_W_q, dtype=np.float32)
    wk = np.asarray(L0_attn_W_k, dtype=np.float32)
    wv = np.asarray(L0_attn_W_v, dtype=np.float32)
    wo = np.asarray(L0_attn_W_o, dtype=np.float32)
    wf = np.ascontiguousarray(np.asarray(L0_ffn_weight, dtype=np.float32).T)
    alpha = np.asarray(L0_ffn_alpha, dtype=np.float32)
    bias = np.asarray(L0_ffn_bias, dtype=np.float32)

    wqkv = np.ascontiguousarray(np.concatenate([wq, wk, wv], axis=1))
    wob = np.ascontiguousarray(np.concatenate([wo, bias[None, :]], axis=0))
    alpha_rep = np.ascontiguousarray(
        np.broadcast_to(alpha.reshape(1, 1), (P, 1)).astype(np.float32))

    ident = np.eye(P, dtype=np.float32).astype(ml_dtypes.bfloat16)
    maskt = np.triu(np.ones((P, P), dtype=np.float32)).astype(ml_dtypes.bfloat16)

    if _NC_CACHE is None:
        _NC_CACHE = build_nc()
    nc = _NC_CACHE

    in_maps = []
    for i in range(B):
        in_maps.append({
            "x": np.ascontiguousarray(x[i]),
            "wf": wf,
            "wqkv": wqkv,
            "wob": wob,
            "alpha": alpha_rep,
            "ident": ident,
            "maskt": maskt,
        })
    res = run_bass_kernel_spmd(nc, in_maps, core_ids=list(range(B)))
    out = np.stack([np.asarray(res.results[i]["out"]) for i in range(B)], axis=0)
    return out.astype(np.float32)


# version 20
# speedup vs baseline: 1.0088x; 1.0070x over previous
import numpy as np
import ml_dtypes
from contextlib import ExitStack

import concourse.bass as bass
import concourse.mybir as mybir
from concourse.bass_utils import run_bass_kernel_spmd

F32 = mybir.dt.float32
BF16 = mybir.dt.bfloat16
AF = mybir.ActivationFunctionType
ALU = mybir.AluOpType
AX = mybir.AxisListType

P = 128
D = 1024
H = 64
T = 2048
NT = T // P
NC = D // P
B = 8

F1 = np.float32

DEBUG = False

OFF = {"A1": 0, "A2": 1, "A3": 1, "A4": 2, "A5": 3,
       "T1": 4, "G1": 4, "T2": 4, "G2": 4,
       "QK": 5, "PH": 6, "TR": 7, "SE": 7, "AM": 8,
       "OM": 9, "ON": 10, "OW": 11, "E": 12, "EO": 13}


class Trk:

    def __init__(self):
        self.E = {}
        self.planning = True
        self.n = 0

    def reset(self, planning):
        self.planning = planning
        self.n = 0

    def ev(self, key):
        self.n += 1
        if self.planning:
            self.E[key] = self.n
        else:
            assert self.E[key] == self.n, (key, self.E[key], self.n)
        return self.n


def make_sched():
    wabs_at = {2: [0, 1, 2], 3: [3, 4, 5], 4: [6, 7]}
    wtern_at = {6: [0, 4], 7: [1, 5], 8: [2, 6], 9: [3, 7]}
    wsub_at = {8: [0, 4], 9: [1, 5], 10: [2, 6], 11: [3, 7]}
    iters = [[] for _ in range(NT + 17)]
    for t in range(NT):
        for s, o in OFF.items():
            iters[t + o].append((s, t))
    for i, cs in wabs_at.items():
        for c in cs:
            iters[i].append(("Wabs", c))
    iters[5].append(("Wdelta", 0))
    for i, cs in wtern_at.items():
        for c in cs:
            iters[i].append(("Wtern", c))
    for i, cs in wsub_at.items():
        for c in cs:
            iters[i].append(("Wsub", c))
    return iters


def build_nc():
    nc = bass.Bass("TRN2", target_bir_lowering=False)
    dbg_d = {}
    if DEBUG:
        dbg_d["dbg_wT"] = nc.dram_tensor("dbg_wT", [P, NC * D], BF16, kind="ExternalOutput").ap()
        dbg_d["dbg_xnfm"] = nc.dram_tensor("dbg_xnfm", [P, NT * D], BF16, kind="ExternalOutput").ap()
        dbg_d["dbg_o1T"] = nc.dram_tensor("dbg_o1T", [H + 1, NT * P], BF16, kind="ExternalOutput").ap()
        dbg_d["dbg_draw"] = nc.dram_tensor("dbg_draw", [P, 1], F32, kind="ExternalOutput").ap()
        dbg_d["dbg_stats"] = nc.dram_tensor("dbg_stats", [P, 48], F32, kind="ExternalOutput").ap()

    x_d = nc.dram_tensor("x", [T, D], F32, kind="ExternalInput").ap()
    wf_d = nc.dram_tensor("wf", [D, D], F32, kind="ExternalInput").ap()
    wqkv_d = nc.dram_tensor("wqkv", [D, 3 * H], F32, kind="ExternalInput").ap()
    wob_d = nc.dram_tensor("wob", [H + 1, D], F32, kind="ExternalInput").ap()
    alpha_d = nc.dram_tensor("alpha", [P, 1], F32, kind="ExternalInput").ap()
    id_d = nc.dram_tensor("ident", [P, P], BF16, kind="ExternalInput").ap()
    mask_d = nc.dram_tensor("maskt", [P, P], BF16, kind="ExternalInput").ap()
    out_d = nc.dram_tensor("out", [T, D], F32, kind="ExternalOutput").ap()

    trk = {"te": Trk(), "dve": Trk(), "sc": Trk(), "gp": Trk()}
    iters = make_sched()
    PRIO = {
        "te": {"T1": 1, "QK": 2, "TR": 3, "AM": 4, "OM": 5, "ON": 6,
               "T2": 7, "Wdelta": 8, "OW": 9},
        "dve": {"ON": 0, "PH": 1, "G1": 2, "E": 3, "AM": 4,
                "A5": 5, "A3": 6, "A1": 7,
                "Wabs": 9, "Wtern": 10, "Wsub": 11, "Wdelta": 12},
        "sc": {"PH": 0, "ON": 1, "OW": 2, "A4": 3, "A2": 4,
               "A1": 5, "G2": 6, "SE": 7, "Wtern": 8, "Wdelta": 9, "Wabs": 10},
        "gp": {"Wdelta": 0, "EO": 1},
    }
    eng_sched = {}
    for eng, pr in PRIO.items():
        seq = []
        for it in iters:
            seq.extend(sorted((ent for ent in it if ent[0] in pr),
                              key=lambda ent: pr[ent[0]]))
        eng_sched[eng] = seq

    with ExitStack() as ctx:
        sb = lambda name, shape, dt: ctx.enter_context(nc.sbuf_tensor(name, shape, dt))
        ps = lambda name, shape: ctx.enter_context(nc.psum_tensor(name, shape, F32))

        x_sb = sb("x_sb", [P, NT * D], F32)
        xnfm = sb("xnfm", [P, NT * D], BF16)
        wsb = sb("wsb", [P, NC * D], F32)
        wT = sb("wT", [P, NC * D], BF16)
        p_sb = sb("p_sb", [P, NC * D], BF16)
        n_sb = sb("n_sb", [P, NC * D], BF16)
        wqkv_sb = sb("wqkv_sb", [P, NC * 3 * H], BF16)
        wob_sb = sb("wob_sb", [H + 1, D], BF16)
        id_sb = sb("id_sb", [P, P], BF16)
        mask_sb = sb("mask_sb", [P, P], BF16)
        alpha_sb = sb("alpha_sb", [P, 1], F32)
        alpha_h = sb("alpha_h", [P, 1], F32)
        xn_sb = sb("xn_sb", [P, 2 * D], BF16)
        qkphi = sb("qkphi", [P, 4 * P], BF16)
        qkT = sb("qkT", [H, 4 * 2 * P], BF16)
        am = sb("am", [P, 4 * P], BF16)
        v1 = sb("v1", [P, 4 * (H + 1)], BF16)
        s1 = sb("s1", [H, 4 * (H + 1)], BF16)
        onb = sb("onb", [P, H], BF16)
        o1T = sb("o1T", [H + 1, NT * P], BF16)
        rn_sb = sb("rn_sb", [P, P], F32)
        e_sb = sb("e_sb", [P, 4 * P], F32)
        junk_d = sb("junk_d", [P, D], F32)
        junk_s = sb("junk_s", [P, D], BF16)
        stats = sb("stats", [P, 48], F32)
        consts = sb("consts", [P, 8], F32)
        absp = sb("absp", [P, NC], F32)
        absphl = sb("absphl", [P, 2 * NC], BF16)
        hi_f = sb("hi_f", [P, NC], F32)
        t16 = sb("t16", [1, 2 * NC], F32)
        tot1 = sb("tot1", [1, 1], F32)
        trow = sb("trow", [1, P], F32)
        draw = sb("draw", [P, 1], F32)
        dpos = sb("dpos", [P, 1], F32)
        dneg = sb("dneg", [P, 1], F32)
        ones_cb = sb("ones_cb", [P, 1], BF16)
        ones_rf = sb("ones_rf", [1, P], F32)
        dbgo_sb = sb("dbgo_sb", [P, H + 1], F32) if DEBUG else None

        pF = [ps("pF0", [P, 512]), ps("pF1", [P, 512])]
        pState = ps("pState", [P, 128])
        pM1 = ps("pM1", [P, 448])
        pMo = ps("pMo", [P, 130])
        pMt = ps("pMt", [P, 128])
        pMa = ps("pMa", [P, 128])
        pT = ps("pT", [P, 512])

        sbq = ctx.enter_context(nc.semaphore("sbq"))
        sdxs = [ctx.enter_context(nc.semaphore(f"sdx{t}")) for t in range(NT)]
        sdws = [ctx.enter_context(nc.semaphore(f"sdw{c}")) for c in range(NC)]
        sdc = ctx.enter_context(nc.semaphore("sdc"))
        sdo = ctx.enter_context(nc.semaphore("sdo"))
        ste = ctx.enter_context(nc.semaphore("ste"))
        sdve = ctx.enter_context(nc.semaphore("sdve"))
        ssc = ctx.enter_context(nc.semaphore("ssc"))
        sgp = ctx.enter_context(nc.semaphore("sgp"))

        C_NM, C_PM, C_E6, C_DS, C_NDS = range(5)
        S_SUM, S_SSQ, S_NEGM, S_T2, S_U, S_LNV, S_RINV, S_DN, S_RR, S_LN2 = range(10)
        N_CONST_DMA = 4 + NC

        def EV(eng, key):
            return trk[eng].E[key]

        def mk_waiter(e):
            lastw = {}

            def w(sem, val):
                if e is None or val is None:
                    return
                if isinstance(val, tuple):
                    val = EV(*val)
                if val > lastw.get(id(sem), -1):
                    lastw[id(sem)] = val
                    e.wait_ge(sem, val)
            return w

        def stc(b, i):
            return stats[:, 12 * b + i: 12 * b + i + 1]

        def xt(t):
            return x_sb[:, t * D:(t + 1) * D]

        def xnfm_c(t, c):
            return xnfm[:, t * D + c * P: t * D + (c + 1) * P]

        sems = {"te": ste, "dve": sdve, "sc": ssc, "gp": sgp}


        def emit_engine(which, e):
            w = mk_waiter(e)
            tk = trk[which]
            sem = sems[which]

            def fin(ins, key, inc=1):
                tk.n += inc - 1
                tk.ev(key)
                if e is not None:
                    ins.then_inc(sem, inc)

            if which == "dve":
                if e is not None:
                    e.memset(consts[:, C_NM:C_NM + 1], -1.0 / D)
                    e.memset(consts[:, C_PM:C_PM + 1], 1.0 / D)
                    e.memset(consts[:, C_E6:C_E6 + 1], 1e-6)
                    e.memset(consts[:, C_DS:C_DS + 1], 0.7 / (D * D))
                    e.memset(consts[:, C_NDS:C_NDS + 1], -0.7 / (D * D))
                    e.memset(ones_cb[:], 1.0)
                    e.memset(ones_rf[:], 1.0)
                    for _bb in range(4):
                        e.memset(v1[:, _bb * (H + 1) + H: _bb * (H + 1) + H + 1], 1.0)
                    e.memset(s1[:, 0:H + 1], 0.0)
                    ins = e.memset(o1T[H:H + 1, :], 1.0)
                fin(ins if e is not None else None, "prep")
            if which == "gp":
                if e is not None:
                    for c in range(NC):
                        e.dma_start(out=wqkv_sb[:, c * 3 * H:(c + 1) * 3 * H],
                                    in_=wqkv_d[c * P:(c + 1) * P, :]).then_inc(sdc, 16)
                    e.dma_start(out=wob_sb[:], in_=wob_d).then_inc(sdc, 16)

            for ent in eng_sched[which]:
                st, t = ent
                b = t % 2
                b4 = t % 4
                if st == "A1":
                    if which == "dve":
                        w(sdxs[t], 16)
                        ins = None if e is None else e.tensor_scalar(
                            junk_d[:], xt(t), 0.0, 0.0, ALU.add, ALU.add,
                            accum_out=stc(b4, S_SUM))
                        fin(ins, f"sum{t}")
                    elif which == "sc":
                        w(sdxs[t], 16)
                        ins = None if e is None else e.activation(
                            junk_s[:], xt(t), AF.Square, accum_out=stc(b4, S_SSQ))
                        fin(ins, f"ssq{t}")

                elif st == "A2":
                    if which == "sc":
                        w(sdve, ("dve", f"sum{t}"))
                        if e is not None:
                            e.activation(stc(b4, S_NEGM), stc(b4, S_SUM), AF.Identity,
                                         scale=consts[:, C_NM:C_NM + 1])
                            ins = e.activation(stc(b4, S_T2), stc(b4, S_SSQ),
                                               AF.Identity, scale=consts[:, C_PM:C_PM + 1])
                        fin(ins if e is not None else None, f"nm{t}", 2)

                elif st == "A3":
                    if which == "dve":
                        w(ssc, ("sc", f"nm{t}"))
                        ins = None if e is None else e.scalar_tensor_tensor(
                            stc(b4, S_U), stc(b4, S_NEGM), stc(b4, S_NEGM),
                            stc(b4, S_T2), ALU.mult, ALU.subtract)
                        fin(ins, f"u{t}")

                elif st == "A4":
                    if which == "sc":
                        w(sdve, ("dve", f"u{t}"))
                        if e is not None:
                            e.activation(stc(b4, S_LNV), stc(b4, S_U), AF.Ln,
                                         scale=F1(-1.0), bias=F1(1e-5))
                            ins = e.activation(stc(b4, S_RINV), stc(b4, S_LNV),
                                               AF.Exp, scale=F1(-0.5))
                        fin(ins if e is not None else None, f"rinv{t}", 2)

                elif st == "A5":
                    if which == "dve":
                        w(ssc, ("sc", f"rinv{t}"))
                        if t >= 2:
                            w(ste, ("te", f"trg2_{t - 2}"))
                        ins = None if e is None else e.tensor_scalar(
                            xn_sb[:, b * D:(b + 1) * D], xt(t), stc(b4, S_NEGM),
                            stc(b4, S_RINV), ALU.add, ALU.mult)
                        fin(ins, f"xn{t}")

                elif st == "T1":
                    if which == "te":
                        w(sdve, ("dve", f"xn{t}"))
                        w(sdc, 16 * N_CONST_DMA)
                        if t >= 1:
                            w(ssc, ("sc", f"egB2_{t - 1}"))
                        for s_ in range(4):
                            if e is not None:
                                ins = e.matmul(pT[:, s_ * P:(s_ + 1) * P],
                                               xn_sb[:, b * D + s_ * P: b * D + (s_ + 1) * P],
                                               id_sb[:], start=True, stop=True)
                        fin(ins if e is not None else None, f"trg1_{t}")

                elif st == "G1":
                    if which == "dve":
                        w(ste, ("te", f"trg1_{t}"))
                        ins = None if e is None else e.tensor_copy(
                            xnfm[:, t * D: t * D + 512], pT[:, 0:512])
                        fin(ins, f"egA1_{t}")

                elif st == "T2":
                    if which == "te":
                        w(sdve, ("dve", f"egA1_{t}"))
                        for s_ in range(4):
                            if e is not None:
                                ins = e.matmul(pT[:, s_ * P:(s_ + 1) * P],
                                               xn_sb[:, b * D + (4 + s_) * P: b * D + (5 + s_) * P],
                                               id_sb[:], start=True, stop=True)
                        fin(ins if e is not None else None, f"trg2_{t}")

                elif st == "G2":
                    if which == "sc":
                        w(ste, ("te", f"trg2_{t}"))
                        ins = None if e is None else e.activation(
                            xnfm[:, t * D + 512: t * D + 1024], pT[:, 0:512],
                            AF.Identity)
                        fin(ins, f"egB2_{t}")

                elif st == "QK":
                    if which == "te":
                        w(ssc, ("sc", f"egB2_{t}"))
                        w(sdc, 16 * N_CONST_DMA)
                        if t >= 1:
                            w(ssc, ("sc", f"e{t - 1}"))
                            w(ssc, ("sc", f"v{t - 1}"))
                            w(sdve, ("dve", f"phi{t - 1}"))
                        for c in range(NC):
                            if e is not None:
                                ins = e.matmul(pM1[:, 0:3 * H], xnfm_c(t, c),
                                               wqkv_sb[:, c * 3 * H:(c + 1) * 3 * H],
                                               start=(c == 0), stop=(c == NC - 1))
                        fin(ins if e is not None else None, f"qkv{t}")

                elif st == "PH":
                    if which == "sc":
                        w(ste, ("te", f"qkv{t}"))
                        if t >= 2:
                            w(ste, ("te", f"qkT{t - 2}"))
                        if t >= 1:
                            w(sdve, ("dve", f"phi{t - 1}"))
                        if e is not None:
                            e.activation(rn_sb[:], pM1[:, 0:P], AF.Relu, scale=F1(-1.0))
                            ins = e.activation(e_sb[:, b4 * P:(b4 + 1) * P], rn_sb[:],
                                               AF.Exp, scale=F1(-1.0))
                        fin(ins if e is not None else None, f"e{t}", 2)
                        ins = None if e is None else e.activation(
                            v1[:, b4 * (H + 1): b4 * (H + 1) + H], pM1[:, P:P + H],
                            AF.Identity)
                        fin(ins, f"v{t}")
                    elif which == "dve":
                        w(ssc, ("sc", f"e{t}"))
                        w(ssc, ("sc", f"v{t}"))
                        if t >= 2:
                            w(ssc, ("sc", f"qkTe{t - 2}"))
                        ins = None if e is None else e.scalar_tensor_tensor(
                            qkphi[:, b4 * P:(b4 + 1) * P], pM1[:, 0:P], 0.0,
                            e_sb[:, b4 * P:(b4 + 1) * P], ALU.max, ALU.add)
                        fin(ins, f"phi{t}")

                elif st == "TR":
                    if which == "te":
                        w(sdve, ("dve", f"phi{t}"))
                        w(ssc, ("sc", f"v{t}"))
                        if t + 1 < NT:
                            w(ssc, ("sc", f"e{t + 1}"))
                            w(ssc, ("sc", f"v{t + 1}"))
                            w(sdve, ("dve", f"phi{t + 1}"))
                        if t >= 1:
                            w(ssc, ("sc", f"qkTe{t - 1}"))
                        if e is not None:
                            e.matmul(pM1[0:H, 192:320], qkphi[:, b4 * P: b4 * P + H],
                                     id_sb[:], start=True, stop=True)
                            ins = e.matmul(pM1[0:H, 320:448],
                                           qkphi[:, b4 * P + H:(b4 + 1) * P],
                                           id_sb[:], start=True, stop=True)
                        fin(ins if e is not None else None, f"qkT{t}")
                        w(ssc, ("sc", f"v{t}"))
                        if t >= 1:
                            w(ssc, ("sc", f"se{t}"))
                        else:
                            w(sdve, ("dve", "prep"))
                            w(ssc, ("sc", "t16"))
                        ins = None if e is None else e.matmul(
                            pState[0:H, 0:H + 1], qkphi[:, b4 * P + H:(b4 + 1) * P],
                            v1[:, b4 * (H + 1):(b4 + 1) * (H + 1)],
                            start=(t == 0), stop=(t == NT - 1))
                        fin(ins, f"state{t}")

                elif st == "SE":
                    if which == "sc":
                        w(ste, ("te", f"qkT{t}"))
                        if t + 1 < NT:
                            w(sdve, ("dve", f"phi{t + 1}"))
                        ins = None if e is None else e.activation(
                            qkT[:, b4 * 2 * P:(b4 + 1) * 2 * P], pM1[0:H, 192:448],
                            AF.Identity)
                        fin(ins, f"qkTe{t}")
                        w(ste, ("te", f"state{t}"))
                        if t + 1 < NT:
                            bn4 = (t + 1) % 4
                            ins = None if e is None else e.activation(
                                s1[:, bn4 * (H + 1):(bn4 + 1) * (H + 1)],
                                pState[0:H, 0:H + 1], AF.Identity)
                            fin(ins, f"se{t + 1}")

                elif st == "AM":
                    if which == "te":
                        w(ssc, ("sc", f"qkTe{t}"))
                        if t >= 1:
                            w(sdve, ("dve", f"am{t - 1}"))
                        ins = None if e is None else e.matmul(
                            pMa[:, 0:P], qkT[:, b4 * 2 * P + P: b4 * 2 * P + 2 * P],
                            qkT[:, b4 * 2 * P: b4 * 2 * P + P], start=True, stop=True)
                        fin(ins, f"A{t}")
                    elif which == "dve":
                        w(ste, ("te", f"A{t}"))
                        ins = None if e is None else e.tensor_tensor(
                            am[:, b4 * P:(b4 + 1) * P], pMa[:, 0:P], mask_sb[:],
                            ALU.mult)
                        fin(ins, f"am{t}")

                elif st == "OM":
                    ob = (t % 2) * 65
                    if which == "te":
                        w(sdve, ("dve", f"am{t}"))
                        if t >= 1:
                            w(ssc, ("sc", f"se{t}"))
                        else:
                            w(sdve, ("dve", "prep"))
                        if t >= 1:
                            w(sdve, ("dve", f"r{t - 1}"))
                            w(ssc, ("sc", f"on{t - 1}"))
                        if e is not None:
                            e.matmul(pMo[:, ob:ob + H + 1], am[:, b4 * P:(b4 + 1) * P],
                                     v1[:, b4 * (H + 1):(b4 + 1) * (H + 1)],
                                     start=True, stop=False)
                            ins = e.matmul(pMo[:, ob:ob + H + 1],
                                           qkT[:, b4 * 2 * P: b4 * 2 * P + P],
                                           s1[:, b4 * (H + 1):(b4 + 1) * (H + 1)],
                                           start=False, stop=True)
                        fin(ins if e is not None else None, f"o{t}")

                elif st == "ON":
                    ob = (t % 2) * 65
                    if which == "dve":
                        w(ste, ("te", f"o{t}"))
                        if t >= 1:
                            w(ssc, ("sc", f"on{t - 1}"))
                        ins = None if e is None else e.tensor_scalar(
                            stc(b4, S_DN), pMo[:, ob + H:ob + H + 1],
                            consts[:, C_E6:C_E6 + 1], None, ALU.max)
                        fin(ins, f"r{t}")
                    elif which == "sc":
                        w(sdve, ("dve", f"r{t}"))
                        if e is not None:
                            e.activation(stc(b4, S_LN2), stc(b4, S_DN), AF.Ln)
                            e.activation(stc(b4, S_RR), stc(b4, S_LN2), AF.Exp,
                                         scale=F1(-1.0))
                            ins = e.activation(onb[:], pMo[:, ob:ob + H], AF.Identity,
                                               scale=stc(b4, S_RR))
                        fin(ins if e is not None else None, f"on{t}", 3)
                    elif which == "te":
                        w(ssc, ("sc", f"on{t}"))
                        if t >= 1:
                            w(ssc, ("sc", f"o1Te{t - 1}"))
                        ins = None if e is None else e.matmul(
                            pMt[0:H, 0:P], onb[:], id_sb[:],
                            start=True, stop=True)
                        fin(ins, f"oT{t}")

                elif st == "OW":
                    if which == "sc":
                        w(ste, ("te", f"oT{t}"))
                        ins = None if e is None else e.activation(
                            o1T[0:H, t * P:(t + 1) * P], pMt[0:H, 0:P],
                            AF.Identity)
                        fin(ins, f"o1Te{t}")
                    elif which == "te":
                        w(ssc, ("sc", f"o1Te{t}"))
                        w(ssc, ("sc", f"egB2_{t}"))
                        w(sdc, 16 * N_CONST_DMA)
                        if t == 0:
                            w(sdve, ("dve", "prep"))
                            w(sdve, ("dve", f"wsub{NC - 1}"))
                        if t >= 1:
                            w(sdve, ("dve", f"ev_{t - 1}"))
                        if e is not None:
                            e.matmul(pF[0][:], o1T[:, t * P:(t + 1) * P],
                                     wob_sb[:, 0:512], start=True, stop=False)
                            e.matmul(pF[1][:], o1T[:, t * P:(t + 1) * P],
                                     wob_sb[:, 512:1024], start=True, stop=False)
                        for c in range(NC):
                            if e is not None:
                                e.matmul(pF[0][:], xnfm_c(t, c),
                                         wT[:, c * D: c * D + 512],
                                         start=False, stop=(c == NC - 1))
                                ins = e.matmul(pF[1][:], xnfm_c(t, c),
                                               wT[:, c * D + 512:(c + 1) * D],
                                               start=False, stop=(c == NC - 1))
                        fin(ins if e is not None else None, f"ffn{t}")

                elif st == "E":
                    if which == "dve":
                        w(ste, ("te", f"ffn{t}"))
                        if e is not None:
                            e.scalar_tensor_tensor(xt(t)[:, 0:512], pF[0][:], 0.0,
                                                   xt(t)[:, 0:512], ALU.bypass, ALU.add)
                            ins = e.scalar_tensor_tensor(xt(t)[:, 512:1024],
                                                         pF[1][:], 0.0,
                                                         xt(t)[:, 512:1024],
                                                         ALU.bypass, ALU.add)
                        fin(ins if e is not None else None, f"ev_{t}", 2)

                elif st == "EO":
                    if which == "gp":
                        w(sdve, ("dve", f"ev_{t}"))
                        if e is not None:
                            e.dma_start(out=out_d[t * P:(t + 1) * P, :],
                                        in_=xt(t)).then_inc(sdo, 16)

                elif st == "Wabs":
                    if t < 4:
                        if which == "dve":
                            w(sdws[t], 16)
                            ins = None if e is None else e.reduce_sum(
                                absp[:, t:t + 1], wsb[:, t * D:(t + 1) * D],
                                axis=AX.X, apply_absolute_value=True)
                            fin(ins, f"abs{t}")
                    else:
                        if which == "sc":
                            w(sdws[t], 16)
                            ins = None if e is None else e.activation(
                                junk_s[:], wsb[:, t * D:(t + 1) * D], AF.Abs,
                                accum_out=absp[:, t:t + 1])
                            fin(ins, f"abs{t}")

                elif st == "Wdelta":
                    if which == "sc":
                        w(sdve, ("dve", "abs3"))
                        if e is not None:
                            e.activation(absphl[:, 0:NC], absp[:], AF.Identity)
                            ins = e.activation(hi_f[:], absphl[:, 0:NC], AF.Identity)
                        fin(ins if e is not None else None, "dhif", 2)
                        w(ste, ("te", "dmm1"))
                        ins = None if e is None else e.activation(
                            t16[:], pState[0:1, 100:116], AF.Identity)
                        fin(ins, "t16")
                        w(sdve, ("dve", "tot1"))
                        ins = None if e is None else e.activation(
                            trow[:], ones_rf[:], AF.Identity, scale=tot1[0:1, 0:1])
                        fin(ins, "trow")
                        w(sbq, 16)
                        w(sdc, 16 * N_CONST_DMA)
                        if e is not None:
                            e.activation(dpos[:], draw[:], AF.Identity,
                                         scale=consts[:, C_DS:C_DS + 1])
                            e.activation(dneg[:], draw[:], AF.Identity,
                                         scale=consts[:, C_NDS:C_NDS + 1])
                            ins = e.activation(alpha_h[:], alpha_sb[:], AF.Identity,
                                               scale=F1(0.5))
                        fin(ins if e is not None else None, "delta", 3)
                    elif which == "gp":
                        w(ssc, ("sc", "dhif"))
                        ins = None if e is None else e.tensor_tensor(
                            absphl[:, NC:2 * NC], absp[:], hi_f[:], ALU.subtract)
                        fin(ins, "dsplit")
                        w(ssc, ("sc", "trow"))
                        if e is not None:
                            e.dma_start(out=draw[:], in_=trow[0:1, 0:P]).then_inc(sbq, 16)
                    elif which == "te":
                        w(sgp, ("gp", "dsplit"))
                        ins = None if e is None else e.matmul(
                            pState[0:1, 100:116], ones_cb[:], absphl[:],
                            start=True, stop=True)
                        fin(ins, "dmm1")
                    elif which == "dve":
                        w(ssc, ("sc", "t16"))
                        ins = None if e is None else e.reduce_sum(
                            tot1[:], t16[:], axis=AX.X)
                        fin(ins, "tot1")

                elif st == "Wtern":
                    c = t
                    if c < 4:
                        if which == "sc":
                            w(sdws[c], 16)
                            if e is not None:
                                e.activation(p_sb[:, c * D:(c + 1) * D],
                                             wsb[:, c * D:(c + 1) * D], AF.Sign,
                                             bias=dneg[:])
                                ins = e.activation(n_sb[:, c * D:(c + 1) * D],
                                                   wsb[:, c * D:(c + 1) * D], AF.Sign,
                                                   bias=dpos[:])
                            fin(ins if e is not None else None, f"pos{c}", 2)
                    else:
                        if which == "dve":
                            w(ssc, ("sc", "delta"))
                            w(sdws[c], 16)
                            if e is not None:
                                e.tensor_scalar(
                                    p_sb[:, c * D:(c + 1) * D], wsb[:, c * D:(c + 1) * D],
                                    dpos[:], alpha_sb[:], ALU.is_gt, ALU.mult)
                                ins = e.tensor_scalar(
                                    n_sb[:, c * D:(c + 1) * D], wsb[:, c * D:(c + 1) * D],
                                    dneg[:], alpha_sb[:], ALU.is_lt, ALU.mult)
                            fin(ins if e is not None else None, f"pos{c}", 2)

                elif st == "Wsub":
                    c = t
                    if which == "dve":
                        if c < 4:
                            w(ssc, ("sc", f"pos{c}"))
                            if e is not None:
                                e.scalar_tensor_tensor(
                                    n_sb[:, c * D:(c + 1) * D],
                                    p_sb[:, c * D:(c + 1) * D], 0.0,
                                    n_sb[:, c * D:(c + 1) * D], ALU.bypass, ALU.add)
                                ins = e.tensor_scalar(
                                    wT[:, c * D:(c + 1) * D],
                                    n_sb[:, c * D:(c + 1) * D],
                                    alpha_h[:], 0.0, ALU.mult, ALU.add)
                            fin(ins if e is not None else None, f"wsub{c}", 2)
                        else:
                            ins = None if e is None else e.tensor_tensor(
                                wT[:, c * D:(c + 1) * D], p_sb[:, c * D:(c + 1) * D],
                                n_sb[:, c * D:(c + 1) * D], ALU.subtract)
                            fin(ins, f"wsub{c}")

            if which == "dve" and DEBUG:
                ins = None if e is None else e.tensor_copy(dbgo_sb[:], pM2[:, 0:H + 1])
                fin(ins, "dbgo")
            if which == "gp":
                n_out = NT
                if DEBUG:
                    w(sdve, ("dve", "dbgo"))
                    if e is not None:
                        for name, src in [("dbg_wT", wT), ("dbg_xnfm", xnfm),
                                          ("dbg_o1T", o1T), ("dbg_draw", draw),
                                          ("dbg_stats", stats)]:
                            e.dma_start(out=dbg_d[name], in_=src[:]).then_inc(sdo, 16)
                    n_out += 5
                if e is not None:
                    e.wait_ge(sdo, 16 * n_out)

        def emit_sync(e):
            if e is None:
                return
            def dx(t_):
                e.dma_start(out=xt(t_), in_=x_d[t_ * P:(t_ + 1) * P, :]).then_inc(sdxs[t_], 16)

            def dw(c):
                e.dma_start(out=wsb[:, c * D:(c + 1) * D],
                            in_=wf_d[c * P:(c + 1) * P, :]).then_inc(sdws[c], 16)

            dx(0)
            dx(1)
            dx(2)
            e.dma_start(out=id_sb[:], in_=id_d).then_inc(sdc, 16)
            e.dma_start(out=mask_sb[:], in_=mask_d).then_inc(sdc, 16)
            e.dma_start(out=alpha_sb[:], in_=alpha_d).then_inc(sdc, 16)
            for t_ in (3, 4, 5):
                dx(t_)
            for c in (0, 1, 2, 3):
                dw(c)
            dx(6)
            for c in (4, 5, 6, 7):
                dw(c)
            for t_ in range(7, NT):
                dx(t_)

        for tkk in trk.values():
            tkk.reset(planning=True)
        for eng in ("gp", "dve", "sc", "te"):
            emit_engine(eng, None)

        for tkk in trk.values():
            tkk.reset(planning=False)
        block = ctx.enter_context(nc.Block())

        @block.sync
        def _(sync):
            emit_sync(sync)

        @block.gpsimd
        def _(gp):
            emit_engine("gp", gp)

        @block.vector
        def _(ve):
            emit_engine("dve", ve)

        @block.scalar
        def _(sc):
            emit_engine("sc", sc)

        @block.tensor
        def _(te):
            emit_engine("te", te)

    return nc


_NC_CACHE = None


def kernel(L0_attn_x, L0_attn_W_q, L0_attn_W_k, L0_attn_W_v, L0_attn_W_o,
           L0_ffn_weight, L0_ffn_alpha, L0_ffn_bias):
    global _NC_CACHE
    x = np.ascontiguousarray(np.asarray(L0_attn_x, dtype=np.float32))
    wq = np.asarray(L0_attn_W_q, dtype=np.float32)
    wk = np.asarray(L0_attn_W_k, dtype=np.float32)
    wv = np.asarray(L0_attn_W_v, dtype=np.float32)
    wo = np.asarray(L0_attn_W_o, dtype=np.float32)
    wf = np.ascontiguousarray(np.asarray(L0_ffn_weight, dtype=np.float32).T)
    alpha = np.asarray(L0_ffn_alpha, dtype=np.float32)
    bias = np.asarray(L0_ffn_bias, dtype=np.float32)

    wqkv = np.ascontiguousarray(np.concatenate([wq, wk, wv], axis=1))
    wob = np.ascontiguousarray(np.concatenate([wo, bias[None, :]], axis=0))
    alpha_rep = np.ascontiguousarray(
        np.broadcast_to(alpha.reshape(1, 1), (P, 1)).astype(np.float32))

    ident = np.eye(P, dtype=np.float32).astype(ml_dtypes.bfloat16)
    maskt = np.triu(np.ones((P, P), dtype=np.float32)).astype(ml_dtypes.bfloat16)

    if _NC_CACHE is None:
        _NC_CACHE = build_nc()
    nc = _NC_CACHE

    in_maps = []
    for i in range(B):
        in_maps.append({
            "x": np.ascontiguousarray(x[i]),
            "wf": wf,
            "wqkv": wqkv,
            "wob": wob,
            "alpha": alpha_rep,
            "ident": ident,
            "maskt": maskt,
        })
    res = run_bass_kernel_spmd(nc, in_maps, core_ids=list(range(B)))
    out = np.stack([np.asarray(res.results[i]["out"]) for i in range(B)], axis=0)
    return out.astype(np.float32)


# version 21
# speedup vs baseline: 1.0212x; 1.0123x over previous
import numpy as np
import ml_dtypes
from contextlib import ExitStack

import concourse.bass as bass
import concourse.mybir as mybir
from concourse.bass_utils import run_bass_kernel_spmd

F32 = mybir.dt.float32
BF16 = mybir.dt.bfloat16
AF = mybir.ActivationFunctionType
ALU = mybir.AluOpType
AX = mybir.AxisListType

P = 128
D = 1024
H = 64
T = 2048
NT = T // P
NC = D // P
B = 8

F1 = np.float32

DEBUG = False

OFF = {"A1": 0, "A2": 1, "A3": 1, "A4": 2, "A5": 3,
       "T1": 4, "G1": 4, "T2": 4, "G2": 4,
       "QK": 5, "PH": 6, "TR": 7, "SE": 7, "AM": 8,
       "OM": 9, "ON": 10, "OW": 11, "E": 12, "EO": 13}


class Trk:

    def __init__(self):
        self.E = {}
        self.planning = True
        self.n = 0

    def reset(self, planning):
        self.planning = planning
        self.n = 0

    def ev(self, key):
        self.n += 1
        if self.planning:
            self.E[key] = self.n
        else:
            assert self.E[key] == self.n, (key, self.E[key], self.n)
        return self.n


def make_sched():
    wabs_at = {2: [0, 1, 2], 3: [3, 4, 5], 4: [6, 7]}
    wtern_at = {6: [0, 4], 7: [1, 5], 8: [2, 6], 9: [3, 7]}
    wsub_at = {8: [0, 4], 9: [1, 5], 10: [2, 6], 11: [3, 7]}
    iters = [[] for _ in range(NT + 17)]
    for t in range(NT):
        for s, o in OFF.items():
            iters[t + o].append((s, t))
    for i, cs in wabs_at.items():
        for c in cs:
            iters[i].append(("Wabs", c))
    iters[5].append(("Wdelta", 0))
    for i, cs in wtern_at.items():
        for c in cs:
            iters[i].append(("Wtern", c))
    for i, cs in wsub_at.items():
        for c in cs:
            iters[i].append(("Wsub", c))
    return iters


def build_nc():
    nc = bass.Bass("TRN2", target_bir_lowering=False)
    dbg_d = {}
    if DEBUG:
        dbg_d["dbg_wT"] = nc.dram_tensor("dbg_wT", [P, NC * D], BF16, kind="ExternalOutput").ap()
        dbg_d["dbg_xnfm"] = nc.dram_tensor("dbg_xnfm", [P, NT * D], BF16, kind="ExternalOutput").ap()
        dbg_d["dbg_o1T"] = nc.dram_tensor("dbg_o1T", [H + 1, NT * P], BF16, kind="ExternalOutput").ap()
        dbg_d["dbg_draw"] = nc.dram_tensor("dbg_draw", [P, 1], F32, kind="ExternalOutput").ap()
        dbg_d["dbg_stats"] = nc.dram_tensor("dbg_stats", [P, 48], F32, kind="ExternalOutput").ap()

    x_d = nc.dram_tensor("x", [T, D], F32, kind="ExternalInput").ap()
    wf_d = nc.dram_tensor("wf", [D, D], F32, kind="ExternalInput").ap()
    wqkv_d = nc.dram_tensor("wqkv", [D, 3 * H], F32, kind="ExternalInput").ap()
    wob_d = nc.dram_tensor("wob", [H + 1, D], F32, kind="ExternalInput").ap()
    alpha_d = nc.dram_tensor("alpha", [P, 1], F32, kind="ExternalInput").ap()
    id_d = nc.dram_tensor("ident", [P, P], BF16, kind="ExternalInput").ap()
    mask_d = nc.dram_tensor("maskt", [P, P], BF16, kind="ExternalInput").ap()
    out_d = nc.dram_tensor("out", [T, D], F32, kind="ExternalOutput").ap()

    trk = {"te": Trk(), "dve": Trk(), "sc": Trk(), "gp": Trk()}
    iters = make_sched()
    PRIO = {
        "te": {"T1": 1, "QK": 2, "TR": 3, "AM": 4, "OM": 5, "ON": 6,
               "T2": 7, "Wdelta": 8, "OW": 9},
        "dve": {"ON": 0, "PH": 1, "G1": 2, "E": 3, "AM": 4,
                "A5": 5, "A3": 6, "A1": 7,
                "Wabs": 9, "Wtern": 10, "Wsub": 11, "Wdelta": 12},
        "sc": {"PH": 0, "ON": 1, "OW": 2, "A4": 3, "A2": 4,
               "A1": 5, "G2": 6, "SE": 7, "Wtern": 8, "Wdelta": 9, "Wabs": 10},
        "gp": {"Wdelta": 0, "EO": 1},
    }
    eng_sched = {}
    for eng, pr in PRIO.items():
        seq = []
        for it in iters:
            seq.extend(sorted((ent for ent in it if ent[0] in pr),
                              key=lambda ent: pr[ent[0]]))
        eng_sched[eng] = seq

    with ExitStack() as ctx:
        sb = lambda name, shape, dt: ctx.enter_context(nc.sbuf_tensor(name, shape, dt))
        ps = lambda name, shape: ctx.enter_context(nc.psum_tensor(name, shape, F32))

        x_sb = sb("x_sb", [P, NT * D], F32)
        xnfm = sb("xnfm", [P, NT * D], BF16)
        wsb = sb("wsb", [P, NC * D], F32)
        wT = sb("wT", [P, NC * D], BF16)
        p_sb = sb("p_sb", [P, NC * D], BF16)
        n_sb = sb("n_sb", [P, NC * D], BF16)
        wqkv_sb = sb("wqkv_sb", [P, NC * 3 * H], BF16)
        wob_sb = sb("wob_sb", [H + 1, D], BF16)
        id_sb = sb("id_sb", [P, P], BF16)
        mask_sb = sb("mask_sb", [P, P], BF16)
        alpha_sb = sb("alpha_sb", [P, 1], F32)
        alpha_h = sb("alpha_h", [P, 1], F32)
        xn_sb = sb("xn_sb", [P, 2 * D], BF16)
        qkphi = sb("qkphi", [P, 4 * P], BF16)
        qkT = sb("qkT", [H, 4 * 2 * P], BF16)
        am = sb("am", [P, 4 * P], BF16)
        v1 = sb("v1", [P, 4 * (H + 1)], BF16)
        s1 = sb("s1", [H, 4 * (H + 1)], BF16)
        onb = sb("onb", [P, H], BF16)
        o1T = sb("o1T", [H + 1, NT * P], BF16)
        rn_sb = sb("rn_sb", [P, P], F32)
        e_sb = sb("e_sb", [P, 4 * P], F32)
        junk_d = sb("junk_d", [P, D], F32)
        junk_s = sb("junk_s", [P, D], BF16)
        stats = sb("stats", [P, 48], F32)
        consts = sb("consts", [P, 8], F32)
        absp = sb("absp", [P, NC], F32)
        absphl = sb("absphl", [P, 2 * NC], BF16)
        hi_f = sb("hi_f", [P, NC], F32)
        t16 = sb("t16", [1, 2 * NC], F32)
        tot1 = sb("tot1", [1, 1], F32)
        trow = sb("trow", [1, P], F32)
        draw = sb("draw", [P, 1], F32)
        dpos = sb("dpos", [P, 1], F32)
        dneg = sb("dneg", [P, 1], F32)
        ones_cb = sb("ones_cb", [P, 1], BF16)
        ones_rf = sb("ones_rf", [1, P], F32)
        dbgo_sb = sb("dbgo_sb", [P, H + 1], F32) if DEBUG else None

        pF = [ps("pF0", [P, 512]), ps("pF1", [P, 512])]
        pState = ps("pState", [P, 128])
        pM1 = ps("pM1", [P, 448])
        pMo = ps("pMo", [P, 130])
        pMt = ps("pMt", [P, 128])
        pMa = ps("pMa", [P, 128])
        pT = ps("pT", [P, 512])

        sbq = ctx.enter_context(nc.semaphore("sbq"))
        sdxs = [ctx.enter_context(nc.semaphore(f"sdx{t}")) for t in range(NT)]
        sdws = [ctx.enter_context(nc.semaphore(f"sdw{c}")) for c in range(NC)]
        sdc = ctx.enter_context(nc.semaphore("sdc"))
        sdo = ctx.enter_context(nc.semaphore("sdo"))
        ste = ctx.enter_context(nc.semaphore("ste"))
        sdve = ctx.enter_context(nc.semaphore("sdve"))
        ssc = ctx.enter_context(nc.semaphore("ssc"))
        sgp = ctx.enter_context(nc.semaphore("sgp"))

        C_NM, C_PM, C_E6, C_DS, C_NDS = range(5)
        S_SUM, S_SSQ, S_NEGM, S_T2, S_U, S_LNV, S_RINV, S_DN, S_RR, S_LN2 = range(10)
        N_CONST_DMA = 4 + NC

        def EV(eng, key):
            return trk[eng].E[key]

        def mk_waiter(e):
            lastw = {}

            def w(sem, val):
                if e is None or val is None:
                    return
                if isinstance(val, tuple):
                    val = EV(*val)
                if val > lastw.get(id(sem), -1):
                    lastw[id(sem)] = val
                    e.wait_ge(sem, val)
            return w

        def stc(b, i):
            return stats[:, 12 * b + i: 12 * b + i + 1]

        def xt(t):
            return x_sb[:, t * D:(t + 1) * D]

        def xnfm_c(t, c):
            return xnfm[:, t * D + c * P: t * D + (c + 1) * P]

        sems = {"te": ste, "dve": sdve, "sc": ssc, "gp": sgp}


        def emit_engine(which, e):
            w = mk_waiter(e)
            tk = trk[which]
            sem = sems[which]

            def fin(ins, key, inc=1):
                tk.n += inc - 1
                tk.ev(key)
                if e is not None:
                    ins.then_inc(sem, inc)

            if which == "dve":
                if e is not None:
                    e.memset(consts[:, C_NM:C_NM + 1], -1.0 / D)
                    e.memset(consts[:, C_PM:C_PM + 1], 1.0 / D)
                    e.memset(consts[:, C_E6:C_E6 + 1], 1e-6)
                    e.memset(consts[:, C_DS:C_DS + 1], 0.7 / (D * D))
                    e.memset(consts[:, C_NDS:C_NDS + 1], -0.7 / (D * D))
                    e.memset(ones_cb[:], 1.0)
                    e.memset(ones_rf[:], 1.0)
                    for _bb in range(4):
                        e.memset(v1[:, _bb * (H + 1) + H: _bb * (H + 1) + H + 1], 1.0)
                    e.memset(s1[:, 0:H + 1], 0.0)
                    ins = e.memset(o1T[H:H + 1, :], 1.0)
                fin(ins if e is not None else None, "prep")
            if which == "sc":
                if e is not None:
                    for c in range(NC):
                        e.dma_start(out=wsb[:, c * D:(c + 1) * D],
                                    in_=wf_d[c * P:(c + 1) * P, :]).then_inc(sdws[c], 16)
            if which == "gp":
                if e is not None:
                    for c in range(NC):
                        e.dma_start(out=wqkv_sb[:, c * 3 * H:(c + 1) * 3 * H],
                                    in_=wqkv_d[c * P:(c + 1) * P, :]).then_inc(sdc, 16)
                    e.dma_start(out=wob_sb[:], in_=wob_d).then_inc(sdc, 16)

            for ent in eng_sched[which]:
                st, t = ent
                b = t % 2
                b4 = t % 4
                if st == "A1":
                    if which == "dve":
                        w(sdxs[t], 16)
                        ins = None if e is None else e.tensor_scalar(
                            junk_d[:], xt(t), 0.0, 0.0, ALU.add, ALU.add,
                            accum_out=stc(b4, S_SUM))
                        fin(ins, f"sum{t}")
                    elif which == "sc":
                        w(sdxs[t], 16)
                        ins = None if e is None else e.activation(
                            junk_s[:], xt(t), AF.Square, accum_out=stc(b4, S_SSQ))
                        fin(ins, f"ssq{t}")

                elif st == "A2":
                    if which == "sc":
                        w(sdve, ("dve", f"sum{t}"))
                        if e is not None:
                            e.activation(stc(b4, S_NEGM), stc(b4, S_SUM), AF.Identity,
                                         scale=consts[:, C_NM:C_NM + 1])
                            ins = e.activation(stc(b4, S_T2), stc(b4, S_SSQ),
                                               AF.Identity, scale=consts[:, C_PM:C_PM + 1])
                        fin(ins if e is not None else None, f"nm{t}", 2)

                elif st == "A3":
                    if which == "dve":
                        w(ssc, ("sc", f"nm{t}"))
                        ins = None if e is None else e.scalar_tensor_tensor(
                            stc(b4, S_U), stc(b4, S_NEGM), stc(b4, S_NEGM),
                            stc(b4, S_T2), ALU.mult, ALU.subtract)
                        fin(ins, f"u{t}")

                elif st == "A4":
                    if which == "sc":
                        w(sdve, ("dve", f"u{t}"))
                        if e is not None:
                            e.activation(stc(b4, S_LNV), stc(b4, S_U), AF.Ln,
                                         scale=F1(-1.0), bias=F1(1e-5))
                            ins = e.activation(stc(b4, S_RINV), stc(b4, S_LNV),
                                               AF.Exp, scale=F1(-0.5))
                        fin(ins if e is not None else None, f"rinv{t}", 2)

                elif st == "A5":
                    if which == "dve":
                        w(ssc, ("sc", f"rinv{t}"))
                        if t >= 2:
                            w(ste, ("te", f"trg2_{t - 2}"))
                        ins = None if e is None else e.tensor_scalar(
                            xn_sb[:, b * D:(b + 1) * D], xt(t), stc(b4, S_NEGM),
                            stc(b4, S_RINV), ALU.add, ALU.mult)
                        fin(ins, f"xn{t}")

                elif st == "T1":
                    if which == "te":
                        w(sdve, ("dve", f"xn{t}"))
                        w(sdc, 16 * N_CONST_DMA)
                        if t >= 1:
                            w(ssc, ("sc", f"egB2_{t - 1}"))
                        for s_ in range(4):
                            if e is not None:
                                ins = e.matmul(pT[:, s_ * P:(s_ + 1) * P],
                                               xn_sb[:, b * D + s_ * P: b * D + (s_ + 1) * P],
                                               id_sb[:], start=True, stop=True)
                        fin(ins if e is not None else None, f"trg1_{t}")

                elif st == "G1":
                    if which == "dve":
                        w(ste, ("te", f"trg1_{t}"))
                        ins = None if e is None else e.tensor_copy(
                            xnfm[:, t * D: t * D + 512], pT[:, 0:512])
                        fin(ins, f"egA1_{t}")

                elif st == "T2":
                    if which == "te":
                        w(sdve, ("dve", f"egA1_{t}"))
                        for s_ in range(4):
                            if e is not None:
                                ins = e.matmul(pT[:, s_ * P:(s_ + 1) * P],
                                               xn_sb[:, b * D + (4 + s_) * P: b * D + (5 + s_) * P],
                                               id_sb[:], start=True, stop=True)
                        fin(ins if e is not None else None, f"trg2_{t}")

                elif st == "G2":
                    if which == "sc":
                        w(ste, ("te", f"trg2_{t}"))
                        ins = None if e is None else e.activation(
                            xnfm[:, t * D + 512: t * D + 1024], pT[:, 0:512],
                            AF.Identity)
                        fin(ins, f"egB2_{t}")

                elif st == "QK":
                    if which == "te":
                        w(ssc, ("sc", f"egB2_{t}"))
                        w(sdc, 16 * N_CONST_DMA)
                        if t >= 1:
                            w(ssc, ("sc", f"e{t - 1}"))
                            w(ssc, ("sc", f"v{t - 1}"))
                            w(sdve, ("dve", f"phi{t - 1}"))
                        for c in range(NC):
                            if e is not None:
                                ins = e.matmul(pM1[:, 0:3 * H], xnfm_c(t, c),
                                               wqkv_sb[:, c * 3 * H:(c + 1) * 3 * H],
                                               start=(c == 0), stop=(c == NC - 1))
                        fin(ins if e is not None else None, f"qkv{t}")

                elif st == "PH":
                    if which == "sc":
                        w(ste, ("te", f"qkv{t}"))
                        if t >= 2:
                            w(ste, ("te", f"qkT{t - 2}"))
                        if t >= 1:
                            w(sdve, ("dve", f"phi{t - 1}"))
                        if e is not None:
                            e.activation(rn_sb[:], pM1[:, 0:P], AF.Relu, scale=F1(-1.0))
                            ins = e.activation(e_sb[:, b4 * P:(b4 + 1) * P], rn_sb[:],
                                               AF.Exp, scale=F1(-1.0))
                        fin(ins if e is not None else None, f"e{t}", 2)
                        ins = None if e is None else e.activation(
                            v1[:, b4 * (H + 1): b4 * (H + 1) + H], pM1[:, P:P + H],
                            AF.Identity)
                        fin(ins, f"v{t}")
                    elif which == "dve":
                        w(ssc, ("sc", f"e{t}"))
                        w(ssc, ("sc", f"v{t}"))
                        if t >= 2:
                            w(ssc, ("sc", f"qkTe{t - 2}"))
                        ins = None if e is None else e.scalar_tensor_tensor(
                            qkphi[:, b4 * P:(b4 + 1) * P], pM1[:, 0:P], 0.0,
                            e_sb[:, b4 * P:(b4 + 1) * P], ALU.max, ALU.add)
                        fin(ins, f"phi{t}")

                elif st == "TR":
                    if which == "te":
                        w(sdve, ("dve", f"phi{t}"))
                        w(ssc, ("sc", f"v{t}"))
                        if t + 1 < NT:
                            w(ssc, ("sc", f"e{t + 1}"))
                            w(ssc, ("sc", f"v{t + 1}"))
                            w(sdve, ("dve", f"phi{t + 1}"))
                        if t >= 1:
                            w(ssc, ("sc", f"qkTe{t - 1}"))
                        if e is not None:
                            e.matmul(pM1[0:H, 192:320], qkphi[:, b4 * P: b4 * P + H],
                                     id_sb[:], start=True, stop=True)
                            ins = e.matmul(pM1[0:H, 320:448],
                                           qkphi[:, b4 * P + H:(b4 + 1) * P],
                                           id_sb[:], start=True, stop=True)
                        fin(ins if e is not None else None, f"qkT{t}")
                        w(ssc, ("sc", f"v{t}"))
                        if t >= 1:
                            w(ssc, ("sc", f"se{t}"))
                        else:
                            w(sdve, ("dve", "prep"))
                            w(ssc, ("sc", "t16"))
                        ins = None if e is None else e.matmul(
                            pState[0:H, 0:H + 1], qkphi[:, b4 * P + H:(b4 + 1) * P],
                            v1[:, b4 * (H + 1):(b4 + 1) * (H + 1)],
                            start=(t == 0), stop=(t == NT - 1))
                        fin(ins, f"state{t}")

                elif st == "SE":
                    if which == "sc":
                        w(ste, ("te", f"qkT{t}"))
                        if t + 1 < NT:
                            w(sdve, ("dve", f"phi{t + 1}"))
                        ins = None if e is None else e.activation(
                            qkT[:, b4 * 2 * P:(b4 + 1) * 2 * P], pM1[0:H, 192:448],
                            AF.Identity)
                        fin(ins, f"qkTe{t}")
                        w(ste, ("te", f"state{t}"))
                        if t + 1 < NT:
                            bn4 = (t + 1) % 4
                            ins = None if e is None else e.activation(
                                s1[:, bn4 * (H + 1):(bn4 + 1) * (H + 1)],
                                pState[0:H, 0:H + 1], AF.Identity)
                            fin(ins, f"se{t + 1}")

                elif st == "AM":
                    if which == "te":
                        w(ssc, ("sc", f"qkTe{t}"))
                        if t >= 1:
                            w(sdve, ("dve", f"am{t - 1}"))
                        ins = None if e is None else e.matmul(
                            pMa[:, 0:P], qkT[:, b4 * 2 * P + P: b4 * 2 * P + 2 * P],
                            qkT[:, b4 * 2 * P: b4 * 2 * P + P], start=True, stop=True)
                        fin(ins, f"A{t}")
                    elif which == "dve":
                        w(ste, ("te", f"A{t}"))
                        ins = None if e is None else e.tensor_tensor(
                            am[:, b4 * P:(b4 + 1) * P], pMa[:, 0:P], mask_sb[:],
                            ALU.mult)
                        fin(ins, f"am{t}")

                elif st == "OM":
                    ob = (t % 2) * 65
                    if which == "te":
                        w(sdve, ("dve", f"am{t}"))
                        if t >= 1:
                            w(ssc, ("sc", f"se{t}"))
                        else:
                            w(sdve, ("dve", "prep"))
                        if t >= 1:
                            w(sdve, ("dve", f"r{t - 1}"))
                            w(ssc, ("sc", f"on{t - 1}"))
                        if e is not None:
                            e.matmul(pMo[:, ob:ob + H + 1], am[:, b4 * P:(b4 + 1) * P],
                                     v1[:, b4 * (H + 1):(b4 + 1) * (H + 1)],
                                     start=True, stop=False)
                            ins = e.matmul(pMo[:, ob:ob + H + 1],
                                           qkT[:, b4 * 2 * P: b4 * 2 * P + P],
                                           s1[:, b4 * (H + 1):(b4 + 1) * (H + 1)],
                                           start=False, stop=True)
                        fin(ins if e is not None else None, f"o{t}")

                elif st == "ON":
                    ob = (t % 2) * 65
                    if which == "dve":
                        w(ste, ("te", f"o{t}"))
                        if t >= 1:
                            w(ssc, ("sc", f"on{t - 1}"))
                        ins = None if e is None else e.tensor_scalar(
                            stc(b4, S_DN), pMo[:, ob + H:ob + H + 1],
                            consts[:, C_E6:C_E6 + 1], None, ALU.max)
                        fin(ins, f"r{t}")
                    elif which == "sc":
                        w(sdve, ("dve", f"r{t}"))
                        if e is not None:
                            e.activation(stc(b4, S_LN2), stc(b4, S_DN), AF.Ln)
                            e.activation(stc(b4, S_RR), stc(b4, S_LN2), AF.Exp,
                                         scale=F1(-1.0))
                            ins = e.activation(onb[:], pMo[:, ob:ob + H], AF.Identity,
                                               scale=stc(b4, S_RR))
                        fin(ins if e is not None else None, f"on{t}", 3)
                    elif which == "te":
                        w(ssc, ("sc", f"on{t}"))
                        if t >= 1:
                            w(ssc, ("sc", f"o1Te{t - 1}"))
                        ins = None if e is None else e.matmul(
                            pMt[0:H, 0:P], onb[:], id_sb[:],
                            start=True, stop=True)
                        fin(ins, f"oT{t}")

                elif st == "OW":
                    if which == "sc":
                        w(ste, ("te", f"oT{t}"))
                        ins = None if e is None else e.activation(
                            o1T[0:H, t * P:(t + 1) * P], pMt[0:H, 0:P],
                            AF.Identity)
                        fin(ins, f"o1Te{t}")
                    elif which == "te":
                        w(ssc, ("sc", f"o1Te{t}"))
                        w(ssc, ("sc", f"egB2_{t}"))
                        w(sdc, 16 * N_CONST_DMA)
                        if t == 0:
                            w(sdve, ("dve", "prep"))
                            w(sdve, ("dve", f"wsub{NC - 1}"))
                        if t >= 1:
                            w(sdve, ("dve", f"ev_{t - 1}"))
                        if e is not None:
                            e.matmul(pF[0][:], o1T[:, t * P:(t + 1) * P],
                                     wob_sb[:, 0:512], start=True, stop=False)
                            e.matmul(pF[1][:], o1T[:, t * P:(t + 1) * P],
                                     wob_sb[:, 512:1024], start=True, stop=False)
                        for c in range(NC):
                            if e is not None:
                                e.matmul(pF[0][:], xnfm_c(t, c),
                                         wT[:, c * D: c * D + 512],
                                         start=False, stop=(c == NC - 1))
                                ins = e.matmul(pF[1][:], xnfm_c(t, c),
                                               wT[:, c * D + 512:(c + 1) * D],
                                               start=False, stop=(c == NC - 1))
                        fin(ins if e is not None else None, f"ffn{t}")

                elif st == "E":
                    if which == "dve":
                        w(ste, ("te", f"ffn{t}"))
                        if e is not None:
                            e.scalar_tensor_tensor(xt(t)[:, 0:512], pF[0][:], 0.0,
                                                   xt(t)[:, 0:512], ALU.bypass, ALU.add)
                            ins = e.scalar_tensor_tensor(xt(t)[:, 512:1024],
                                                         pF[1][:], 0.0,
                                                         xt(t)[:, 512:1024],
                                                         ALU.bypass, ALU.add)
                        fin(ins if e is not None else None, f"ev_{t}", 2)

                elif st == "EO":
                    if which == "gp":
                        w(sdve, ("dve", f"ev_{t}"))
                        if e is not None:
                            e.dma_start(out=out_d[t * P:(t + 1) * P, :],
                                        in_=xt(t)).then_inc(sdo, 16)

                elif st == "Wabs":
                    if t < 4:
                        if which == "dve":
                            w(sdws[t], 16)
                            ins = None if e is None else e.reduce_sum(
                                absp[:, t:t + 1], wsb[:, t * D:(t + 1) * D],
                                axis=AX.X, apply_absolute_value=True)
                            fin(ins, f"abs{t}")
                    else:
                        if which == "sc":
                            w(sdws[t], 16)
                            ins = None if e is None else e.activation(
                                junk_s[:], wsb[:, t * D:(t + 1) * D], AF.Abs,
                                accum_out=absp[:, t:t + 1])
                            fin(ins, f"abs{t}")

                elif st == "Wdelta":
                    if which == "sc":
                        w(sdve, ("dve", "abs3"))
                        if e is not None:
                            e.activation(absphl[:, 0:NC], absp[:], AF.Identity)
                            ins = e.activation(hi_f[:], absphl[:, 0:NC], AF.Identity)
                        fin(ins if e is not None else None, "dhif", 2)
                        w(ste, ("te", "dmm1"))
                        ins = None if e is None else e.activation(
                            t16[:], pState[0:1, 100:116], AF.Identity)
                        fin(ins, "t16")
                        w(sdve, ("dve", "tot1"))
                        ins = None if e is None else e.activation(
                            trow[:], ones_rf[:], AF.Identity, scale=tot1[0:1, 0:1])
                        fin(ins, "trow")
                        w(sbq, 16)
                        w(sdc, 16 * N_CONST_DMA)
                        if e is not None:
                            e.activation(dpos[:], draw[:], AF.Identity,
                                         scale=consts[:, C_DS:C_DS + 1])
                            e.activation(dneg[:], draw[:], AF.Identity,
                                         scale=consts[:, C_NDS:C_NDS + 1])
                            ins = e.activation(alpha_h[:], alpha_sb[:], AF.Identity,
                                               scale=F1(0.5))
                        fin(ins if e is not None else None, "delta", 3)
                    elif which == "gp":
                        w(ssc, ("sc", "dhif"))
                        ins = None if e is None else e.tensor_tensor(
                            absphl[:, NC:2 * NC], absp[:], hi_f[:], ALU.subtract)
                        fin(ins, "dsplit")
                        w(ssc, ("sc", "trow"))
                        if e is not None:
                            e.dma_start(out=draw[:], in_=trow[0:1, 0:P]).then_inc(sbq, 16)
                    elif which == "te":
                        w(sgp, ("gp", "dsplit"))
                        ins = None if e is None else e.matmul(
                            pState[0:1, 100:116], ones_cb[:], absphl[:],
                            start=True, stop=True)
                        fin(ins, "dmm1")
                    elif which == "dve":
                        w(ssc, ("sc", "t16"))
                        ins = None if e is None else e.reduce_sum(
                            tot1[:], t16[:], axis=AX.X)
                        fin(ins, "tot1")

                elif st == "Wtern":
                    c = t
                    if c < 4:
                        if which == "sc":
                            w(sdws[c], 16)
                            if e is not None:
                                e.activation(p_sb[:, c * D:(c + 1) * D],
                                             wsb[:, c * D:(c + 1) * D], AF.Sign,
                                             bias=dneg[:])
                                ins = e.activation(n_sb[:, c * D:(c + 1) * D],
                                                   wsb[:, c * D:(c + 1) * D], AF.Sign,
                                                   bias=dpos[:])
                            fin(ins if e is not None else None, f"pos{c}", 2)
                    else:
                        if which == "dve":
                            w(ssc, ("sc", "delta"))
                            w(sdws[c], 16)
                            if e is not None:
                                e.tensor_scalar(
                                    p_sb[:, c * D:(c + 1) * D], wsb[:, c * D:(c + 1) * D],
                                    dpos[:], alpha_sb[:], ALU.is_gt, ALU.mult)
                                ins = e.tensor_scalar(
                                    n_sb[:, c * D:(c + 1) * D], wsb[:, c * D:(c + 1) * D],
                                    dneg[:], alpha_sb[:], ALU.is_lt, ALU.mult)
                            fin(ins if e is not None else None, f"pos{c}", 2)

                elif st == "Wsub":
                    c = t
                    if which == "dve":
                        if c < 4:
                            w(ssc, ("sc", f"pos{c}"))
                            if e is not None:
                                e.scalar_tensor_tensor(
                                    n_sb[:, c * D:(c + 1) * D],
                                    p_sb[:, c * D:(c + 1) * D], 0.0,
                                    n_sb[:, c * D:(c + 1) * D], ALU.bypass, ALU.add)
                                ins = e.tensor_scalar(
                                    wT[:, c * D:(c + 1) * D],
                                    n_sb[:, c * D:(c + 1) * D],
                                    alpha_h[:], 0.0, ALU.mult, ALU.add)
                            fin(ins if e is not None else None, f"wsub{c}", 2)
                        else:
                            ins = None if e is None else e.tensor_tensor(
                                wT[:, c * D:(c + 1) * D], p_sb[:, c * D:(c + 1) * D],
                                n_sb[:, c * D:(c + 1) * D], ALU.subtract)
                            fin(ins, f"wsub{c}")

            if which == "dve" and DEBUG:
                ins = None if e is None else e.tensor_copy(dbgo_sb[:], pM2[:, 0:H + 1])
                fin(ins, "dbgo")
            if which == "gp":
                n_out = NT
                if DEBUG:
                    w(sdve, ("dve", "dbgo"))
                    if e is not None:
                        for name, src in [("dbg_wT", wT), ("dbg_xnfm", xnfm),
                                          ("dbg_o1T", o1T), ("dbg_draw", draw),
                                          ("dbg_stats", stats)]:
                            e.dma_start(out=dbg_d[name], in_=src[:]).then_inc(sdo, 16)
                    n_out += 5
                if e is not None:
                    e.wait_ge(sdo, 16 * n_out)

        def emit_sync(e):
            if e is None:
                return
            for t_ in range(2):
                e.dma_start(out=xt(t_), in_=x_d[t_ * P:(t_ + 1) * P, :]).then_inc(sdxs[t_], 16)
            e.dma_start(out=id_sb[:], in_=id_d).then_inc(sdc, 16)
            e.dma_start(out=mask_sb[:], in_=mask_d).then_inc(sdc, 16)
            e.dma_start(out=alpha_sb[:], in_=alpha_d).then_inc(sdc, 16)
            for t_ in range(2, NT):
                e.dma_start(out=xt(t_), in_=x_d[t_ * P:(t_ + 1) * P, :]).then_inc(sdxs[t_], 16)

        for tkk in trk.values():
            tkk.reset(planning=True)
        for eng in ("gp", "dve", "sc", "te"):
            emit_engine(eng, None)

        for tkk in trk.values():
            tkk.reset(planning=False)
        block = ctx.enter_context(nc.Block())

        @block.sync
        def _(sync):
            emit_sync(sync)

        @block.gpsimd
        def _(gp):
            emit_engine("gp", gp)

        @block.vector
        def _(ve):
            emit_engine("dve", ve)

        @block.scalar
        def _(sc):
            emit_engine("sc", sc)

        @block.tensor
        def _(te):
            emit_engine("te", te)

    return nc


_NC_CACHE = None


def kernel(L0_attn_x, L0_attn_W_q, L0_attn_W_k, L0_attn_W_v, L0_attn_W_o,
           L0_ffn_weight, L0_ffn_alpha, L0_ffn_bias):
    global _NC_CACHE
    x = np.ascontiguousarray(np.asarray(L0_attn_x, dtype=np.float32))
    wq = np.asarray(L0_attn_W_q, dtype=np.float32)
    wk = np.asarray(L0_attn_W_k, dtype=np.float32)
    wv = np.asarray(L0_attn_W_v, dtype=np.float32)
    wo = np.asarray(L0_attn_W_o, dtype=np.float32)
    wf = np.ascontiguousarray(np.asarray(L0_ffn_weight, dtype=np.float32).T)
    alpha = np.asarray(L0_ffn_alpha, dtype=np.float32)
    bias = np.asarray(L0_ffn_bias, dtype=np.float32)

    wqkv = np.ascontiguousarray(np.concatenate([wq, wk, wv], axis=1))
    wob = np.ascontiguousarray(np.concatenate([wo, bias[None, :]], axis=0))
    alpha_rep = np.ascontiguousarray(
        np.broadcast_to(alpha.reshape(1, 1), (P, 1)).astype(np.float32))

    ident = np.eye(P, dtype=np.float32).astype(ml_dtypes.bfloat16)
    maskt = np.triu(np.ones((P, P), dtype=np.float32)).astype(ml_dtypes.bfloat16)

    if _NC_CACHE is None:
        _NC_CACHE = build_nc()
    nc = _NC_CACHE

    in_maps = []
    for i in range(B):
        in_maps.append({
            "x": np.ascontiguousarray(x[i]),
            "wf": wf,
            "wqkv": wqkv,
            "wob": wob,
            "alpha": alpha_rep,
            "ident": ident,
            "maskt": maskt,
        })
    res = run_bass_kernel_spmd(nc, in_maps, core_ids=list(range(B)))
    out = np.stack([np.asarray(res.results[i]["out"]) for i in range(B)], axis=0)
    return out.astype(np.float32)
